# version 24
# baseline (speedup 1.0000x reference)
import math
import os
import numpy as np
from contextlib import ExitStack

import concourse.bass as bass
import concourse.mybir as mybir
from concourse.bass_utils import run_bass_kernel_spmd
import ml_dtypes

BF = ml_dtypes.bfloat16
F32 = mybir.dt.float32
BF16 = mybir.dt.bfloat16
AF = mybir.ActivationFunctionType
ALU = mybir.AluOpType
AX = mybir.AxisListType.X

NEG_BIG = -1.0e9


def make_cfg(D=2048, TOWN=1024, W=1024, R=8, n_cores=8):
    c = dict(D=D, TOWN=TOWN, W=W, R=R, n_cores=n_cores)
    c["HD"] = 128
    c["H"] = D // 128
    c["DC"] = D // 128
    c["TCTX"] = TOWN + W
    c["NJT"] = c["TCTX"] // 128
    c["NQT"] = TOWN // 128
    c["MODW"] = 6 * D
    c["MODSH"] = c["MODW"] // n_cores
    c["theta"] = 10000.0
    c["eps"] = 1e-5
    c["EGW"] = 256
    c["HG"] = min(4, c["H"])
    return c


def splits(off, end, step):
    out = []
    a = off
    while a < end:
        b = min(end, (a // step + 1) * step)
        out.append((a, b))
        a = b
    return out


def host_tables(cfg, core):
    TOWN, W, TCTX = cfg["TOWN"], cfg["W"], cfg["TCTX"]
    HD = cfg["HD"]
    t0 = TOWN * core - W
    pos = np.arange(TCTX, dtype=np.float64) + t0
    inv = 1.0 / (cfg["theta"] ** (np.arange(0, HD, 2, dtype=np.float64) / HD))
    fr = pos[:, None] * inv[None, :]
    s4 = float(HD) ** -0.25
    cos = (np.cos(fr) * s4).astype(np.float32).T
    sin = (np.sin(fr) * s4).astype(np.float32).T
    costab = np.ascontiguousarray(np.concatenate([cos, cos], 0))
    sintab = np.ascontiguousarray(np.concatenate([-sin, sin], 0))
    hb = np.zeros((128, cfg["NJT"]), dtype=np.float32)
    if core == 0:
        hb[:, : W // 128] = NEG_BIG
    p = np.arange(128)
    tri_lo = (p[None, :] < p[:, None]).astype(BF)
    tri_hi = (p[None, :] >= p[:, None]).astype(BF)
    return dict(costab=costab, sintab=sintab, halo_bias=hb, tri_lo=tri_lo,
                tri_hi=tri_hi, ident=np.eye(128, dtype=BF),
                identf=np.eye(128, dtype=np.float32),
                ones_col=np.ones((128, 128), dtype=BF))


class SemT:
    def __init__(self, nc, es, name):
        self.h = es.enter_context(nc.semaphore(name))
        self.n = 0


class Prog:
    ENGS = ("tensor", "vector", "scalar", "gpsimd", "sync")
    TAGS = []
    WSEEN = {}
    AUTO = {"vector": "dve", "scalar": "act", "gpsimd": "gp"}

    def __init__(self, nc, esem):
        self.nc = nc
        self.ops = []
        self.esem = esem
        Prog.WSEEN = {}
        Prog.TAGS = []

    def op(self, eng, fn, waits=(), inc=None, inc_k=1, dma=False, tag=""):
        w = [(s, v) for (s, v) in waits if v > 0]
        if eng in self.AUTO:
            sem = self.esem[self.AUTO[eng]]
            if sem.n > 0:
                w.append((sem, sem.n))
            if inc is None and not dma:
                inc, inc_k = sem, 1
        self.ops.append((eng, fn, w, inc, inc_k, tag))
        if inc is not None:
            inc.n += inc_k
            return inc.n
        return None

    def flush(self):
        byeng = {e: [] for e in self.ENGS}
        for o in self.ops:
            byeng[o[0]].append(o)
        with self.nc.Block() as block:
            for ename in self.ENGS:
                ops = byeng[ename]
                if not ops:
                    continue

                def emit(e, ops=ops, ename=ename):
                    seen = Prog.WSEEN.setdefault(ename, {})
                    for (_, fn, waits, inc, inc_k, tag) in ops:
                        n0 = self.nc.n_instructions()
                        for (s, v) in waits:
                            if seen.get(id(s), -1) >= v:
                                continue
                            seen[id(s)] = v
                            e.wait_ge(s.h, v)
                        ins = fn(e)
                        if inc is not None:
                            assert ins is not None, "op with inc returned None"
                            ins.then_inc(inc.h, inc_k)
                        if tag:
                            n1 = self.nc.n_instructions()
                            Prog.TAGS.append((n0, n1, tag))
                getattr(block, ename)(emit)
        self.ops = []


def build_program(cfg):
    D, TOWN, W, TCTX, R = cfg["D"], cfg["TOWN"], cfg["W"], cfg["TCTX"], cfg["R"]
    H, DC = cfg["H"], cfg["DC"]
    NJT, NQT = cfg["NJT"], cfg["NQT"]
    MODSH = cfg["MODSH"]
    EGW, HG = cfg["EGW"], cfg["HG"]
    PTW = min(TOWN, W + 128)
    VEW = HG * 128
    eps = cfg["eps"]

    nc = bass.Bass("TRN2", target_bir_lowering=False)
    es = ExitStack()

    x_in = nc.dram_tensor("x", [TCTX, D], F32, kind="ExternalInput").ap()
    wq_in = nc.dram_tensor("wq", [D, D], F32, kind="ExternalInput").ap()
    wk_in = nc.dram_tensor("wk", [D, D], F32, kind="ExternalInput").ap()
    wv_in = nc.dram_tensor("wv", [D, D], F32, kind="ExternalInput").ap()
    wo_in = nc.dram_tensor("wo", [D, D], F32, kind="ExternalInput").ap()
    fac_in = nc.dram_tensor("factors", [R, D, D], F32, kind="ExternalInput").ap()
    bias_in = nc.dram_tensor("fbias", [1, D], F32, kind="ExternalInput").ap()
    c_in = nc.dram_tensor("cvec", [1, D], F32, kind="ExternalInput").ap()
    wmod_in = nc.dram_tensor("wmod", [D, MODSH], F32, kind="ExternalInput").ap()
    cos_in = nc.dram_tensor("costab", [128, TCTX], F32, kind="ExternalInput").ap()
    sin_in = nc.dram_tensor("sintab", [128, TCTX], F32, kind="ExternalInput").ap()
    hb_in = nc.dram_tensor("halo_bias", [128, NJT], F32, kind="ExternalInput").ap()
    trilo_in = nc.dram_tensor("tri_lo", [128, 128], BF16, kind="ExternalInput").ap()
    trihi_in = nc.dram_tensor("tri_hi", [128, 128], BF16, kind="ExternalInput").ap()
    ident_in = nc.dram_tensor("ident", [128, 128], BF16, kind="ExternalInput").ap()
    identf_in = nc.dram_tensor("identf", [128, 128], F32, kind="ExternalInput").ap()
    ones_in = nc.dram_tensor("ones_col", [128, 128], BF16, kind="ExternalInput").ap()
    y_out = nc.dram_tensor("y", [TOWN, D], F32, kind="ExternalOutput").ap()

    modl_d = nc.dram_tensor("modl_d", [1, MODSH], F32)
    modg_d = nc.dram_tensor("modg_d", [cfg["n_cores"], MODSH], F32)
    rows_d = nc.dram_tensor("rows_d", [4, D], F32)
    dens_d = nc.dram_tensor("dens_d", [H, TOWN], F32)
    ao_d = nc.dram_tensor("ao_d", [D, TOWN], BF16)
    fsum_d = nc.dram_tensor("fsum_d", [D, D], BF16)
    mod_flat = modg_d.ap().rearrange("a b -> (a b)")

    sem_names = [
        "pe", "dve", "act", "gp",
        "s_const", "s_xa", "s_xb", "s_wma", "s_wmb",
        "s_wqa", "s_wqb", "s_wka", "s_wkb", "s_wv",
        "s_fga", "s_fgb", "s_fso", "s_mod", "s_den", "s_denb", "s_aoa", "s_aob", "s_aor",
        "s_wo", "s_xoa", "s_xob", "s_fr", "s_ya", "s_yb", "s_row", "s_rowg", "s_cc", "s_modg",
    ]
    S = {n: SemT(nc, es, n) for n in sem_names}
    pe, dve, act, gp = S["pe"], S["dve"], S["act"], S["gp"]

    def sbuf(stack, name, shape, dt):
        return stack.enter_context(nc.sbuf_tensor(name, shape, dt))

    ident = sbuf(es, "identb", [128, 128], BF16)
    identf = sbuf(es, "identf_s", [128, 128], F32)
    g1c = sbuf(es, "g1c", [128, DC], F32)
    s1c = sbuf(es, "s1c", [128, DC], F32)
    g2c = sbuf(es, "g2c", [128, DC], F32)
    s2c = sbuf(es, "s2c", [128, DC], F32)
    siluc = sbuf(es, "siluc", [128, DC], F32)
    stat = sbuf(es, "stat", [128, 10], F32)

    P = Prog(nc, {"dve": S["dve"], "act": S["act"], "gp": S["gp"]})

    def dma(dst, src):
        return lambda e: e.dma_start(out=dst, in_=src)

    def barrier(engines, waits):
        for eng in engines:
            P.op(eng, lambda e: None, waits=list(waits), dma=True)

    f0 = ExitStack()
    SW = min(512, MODSH)
    NSL = (MODSH + SW - 1) // SW
    ps_mv = f0.enter_context(nc.psum_tensor("ps_mv", [1, 1024], F32))
    ps_tf = f0.enter_context(nc.psum_tensor("ps_tf", [128, DC], F32))
    wm_sl = [sbuf(f0, f"wm{i}", [128, DC * SW], F32) for i in range(2)]
    rowbuf = sbuf(f0, "rowbuf", [1, max(D, MODSH)], F32)
    rowbuf2 = sbuf(f0, "rowbuf2", [1, D], F32)
    ctile = sbuf(f0, "ctile", [DC, 128], F32)
    mslice = sbuf(f0, "mslice", [DC, 128], F32)
    scrr_a = sbuf(f0, "scrr_a", [1, D], F32)
    scrr_b = sbuf(f0, "scrr_b", [1, D], F32)

    for dst, srcap in [
            (ident[:], ident_in), (identf[:], identf_in),
            (ctile[:], c_in.rearrange("u (a p) -> (u a) p", p=128))]:
        P.op("sync", dma(dst, srcap), inc=S["s_const"], inc_k=16)
    v_const = S["s_const"].n

    wm_used = [0, 0]

    v_silu = P.op("scalar", lambda e: e.activation(ctile[:], ctile[:], AF.Silu),
                  waits=[(S["s_const"], v_const)], inc=act)
    v_ct = P.op("tensor", lambda e: e.transpose(ps_tf[:], ctile[:], identf[0:DC, 0:DC]),
                waits=[(act, v_silu), (S["s_const"], v_const)], inc=pe)
    v_sc = P.op("vector", lambda e: e.tensor_copy(siluc[:], ps_tf[:]),
                waits=[(pe, v_ct)], inc=dve)

    prev_evac = 0
    for sl in range(NSL):
        b = sl % 2
        sw0, sw1 = sl * SW, min(MODSH, (sl + 1) * SW)
        wid = sw1 - sw0
        wv_ = [(pe, wm_used[b])] if wm_used[b] else []
        vsl = P.op("sync", dma(
            wm_sl[b][:, 0:DC * wid].rearrange("p (c n) -> p c n", n=wid),
            wmod_in[:, sw0:sw1].rearrange("(c p) n -> p c n", p=128)),
            waits=wv_, inc=S["s_wma" if b == 0 else "s_wmb"], inc_k=16)

        def mv(e, b=b, wid=wid):
            ins = None
            pb = ps_mv[0:1, (b % 2) * 512:(b % 2) * 512 + wid]
            for dc in range(DC):
                ins = e.matmul(pb, siluc[:, dc:dc + 1],
                               wm_sl[b][:, dc * wid:(dc + 1) * wid],
                               start=(dc == 0), stop=(dc == DC - 1))
            return ins
        vmm = P.op("tensor", mv,
                   waits=[(S["s_wma" if b == 0 else "s_wmb"], vsl),
                          (dve, max(v_sc, prev_evac))], inc=pe)
        wm_used[b] = vmm
        prev_evac = P.op("vector", lambda e, sw0=sw0, sw1=sw1, b=b: e.tensor_copy(
            rowbuf[0:1, sw0:sw1], ps_mv[0:1, (b % 2) * 512:(b % 2) * 512 + (sw1 - sw0)]),
            waits=[(pe, vmm)], inc=dve)

    v_modl = P.op("gpsimd", dma(modl_d[:, :], rowbuf[0:1, 0:MODSH]),
                  waits=[(dve, prev_evac)], inc=S["s_modg"], inc_k=16)
    v_cc = P.op("gpsimd", lambda e: e.collective_compute(
        "AllGather", ALU.bypass, replica_groups=[list(range(cfg["n_cores"]))],
        ins=[modl_d.ap()], outs=[modg_d.ap()]),
        waits=[(S["s_modg"], v_modl)], inc=S["s_cc"])

    v_tr_prev = 0
    v_ev_prev = v_sc
    for idx, dst, addc in [(0, s1c, 0.0), (1, g1c, 1.0), (3, s2c, 0.0), (4, g2c, 1.0)]:
        wv_ = [(S["s_cc"], v_cc)]
        if v_tr_prev:
            wv_.append((pe, v_tr_prev))
        vld = P.op("sync", dma(
            mslice[:], mod_flat[idx * D:(idx + 1) * D].rearrange("(a p) -> a p", p=128)),
            waits=wv_, inc=S["s_mod"], inc_k=16)
        wv_ = [(S["s_mod"], vld)]
        if v_ev_prev:
            wv_.append((dve, v_ev_prev))
        v_tr_prev = P.op("tensor", lambda e: e.transpose(
            ps_tf[:], mslice[0:DC, :], identf[0:DC, 0:DC]), waits=wv_, inc=pe)
        if addc != 0.0:
            fn = lambda e, dst=dst, a=addc: e.tensor_scalar(dst[:], ps_tf[:], a, None, ALU.add)
        else:
            fn = lambda e, dst=dst: e.tensor_copy(dst[:], ps_tf[:])
        v_ev_prev = P.op("vector", fn, waits=[(pe, v_tr_prev)], inc=dve)
    v_modcols = v_ev_prev

    v_bias = P.op("sync", dma(rowbuf2[0:1, 0:D], bias_in), inc=S["s_row"], inc_k=16)

    def clamp_row(idx):
        vld = P.op("sync", dma(rowbuf[0:1, 0:D], mod_flat[idx * D:(idx + 1) * D]),
                   waits=[(S["s_cc"], v_cc), (dve, dve.n), (S["s_row"], S["s_row"].n)],
                   inc=S["s_row"], inc_k=16)
        P.op("vector", lambda e: e.tensor_scalar(
            scrr_a[0:1, 0:D], rowbuf[0:1, 0:D], -1.0, None, ALU.mult),
            waits=[(S["s_row"], vld)], inc=dve)
        P.op("vector", lambda e: e.tensor_tensor(
            scrr_a[0:1, 0:D], scrr_a[0:1, 0:D], rowbuf[0:1, 0:D], ALU.max), inc=dve)
        P.op("vector", lambda e: e.tensor_scalar(
            scrr_a[0:1, 0:D], scrr_a[0:1, 0:D], 0.25, None, ALU.max), inc=dve)
        P.op("vector", lambda e: e.tensor_scalar(
            scrr_b[0:1, 0:D], rowbuf[0:1, 0:D], 0.0, None, ALU.is_ge), inc=dve)
        P.op("vector", lambda e: e.tensor_scalar(
            scrr_b[0:1, 0:D], scrr_b[0:1, 0:D], 2.0, -1.0, ALU.mult, ALU.add), inc=dve)
        return P.op("vector", lambda e: e.tensor_tensor(
            rowbuf[0:1, 0:D], scrr_a[0:1, 0:D], scrr_b[0:1, 0:D], ALU.mult), inc=dve)

    vga = clamp_row(2)
    P.op("sync", dma(rows_d[0:1, :], rowbuf[0:1, 0:D]),
         waits=[(dve, vga)], inc=S["s_row"], inc_k=16)
    vgb = clamp_row(5)
    v_gb_st = P.op("sync", dma(rows_d[1:2, :], rowbuf[0:1, 0:D]),
                   waits=[(dve, vgb)], inc=S["s_row"], inc_k=16)
    v_gbb = P.op("vector", lambda e: e.tensor_tensor(
        rowbuf[0:1, 0:D], rowbuf[0:1, 0:D], rowbuf2[0:1, 0:D], ALU.mult),
        waits=[(S["s_row"], v_bias), (S["s_row"], v_gb_st)], inc=dve)
    P.op("sync", dma(rows_d[2:3, :], rowbuf[0:1, 0:D]),
         waits=[(dve, v_gbb)], inc=S["s_row"], inc_k=16)
    v_rows_done = S["s_row"].n

    P.flush()
    f0.close()
    if int(os.environ.get("DK_PHASES", "5")) <= 1:
        es.close()
        return nc

    f12 = ExitStack()
    h_fm = sbuf(f12, "h_fm", [128, DC * TCTX], BF16)
    costab = sbuf(f12, "costab_s", [128, TCTX], F32)
    sintab = sbuf(f12, "sintab_s", [128, TCTX], F32)
    halo_b = sbuf(f12, "halo_b", [128, NJT], F32)
    tri_lo = sbuf(f12, "tri_lo_s", [128, 128], BF16)
    tri_hi = sbuf(f12, "tri_hi_s", [128, 128], BF16)
    ones_c = sbuf(f12, "ones_c", [128, 128], BF16)
    RSW = min(1024, TCTX)
    scr_a = sbuf(f12, "scr_a", [128, RSW], F32)
    scr_b = sbuf(f12, "scr_b", [128, RSW], F32)
    v_sb = sbuf(f12, "v_sb", [128, NJT * VEW], BF16)
    wv_sl = sbuf(f12, "wv_sl", [128, DC * VEW], BF16)
    wvh_f = sbuf(f12, "wvh_f", [128, DC * 256], F32)
    def load_tables():
        tbl_w = [(pe, max(wm_used)), (dve, dve.n), (S["s_row"], v_rows_done),
                 (S["s_mod"], S["s_mod"].n), (S["s_modg"], S["s_modg"].n)]
        for dst, srcap in [
                (costab[:], cos_in), (sintab[:], sin_in), (halo_b[:], hb_in),
                (tri_lo[:], trilo_in), (tri_hi[:], trihi_in), (ones_c[:], ones_in)]:
            P.op("sync", dma(dst, srcap), waits=tbl_w, inc=S["s_const"], inc_k=16)
            tbl_w = []
    v_const2 = S["s_const"].n + 6 * 16
    f1 = ExitStack()
    ps_t = [f1.enter_context(nc.psum_tensor(f"ps_t{i}", [128, 128], BF16)) for i in range(2)]
    ps_v = [f1.enter_context(nc.psum_tensor(f"ps_v{i}", [128, VEW], F32)) for i in range(2)]
    xsl = [sbuf(f1, f"xsl{i}", [128, D], F32) for i in range(2)]
    xnsl = [sbuf(f1, f"xn{i}", [128, D], BF16) for i in range(2)]
    sqscr = sbuf(f1, "sqscr", [128, D], F32)

    def emit_ln(n_tiles, gcol, scol, out_fm, out_stride, x_ap_fn, x_wait_fn,
                x_free_sink, ps_tr, xn_bufs, sq_buf, tile_ready,
                after_tile=None):
        tr_done = [0, 0]
        ev_done = [0, 0]
        xn_free = [0, 0]
        for tt in range(n_tiles):
            b = tt % 2
            xa = x_ap_fn(tt)
            xw = x_wait_fn(tt)
            P.op("vector", lambda e, xa=xa: e.reduce_sum(
                stat[:, 0:1], xa, axis=AX, negate=True), waits=xw, inc=dve)
            P.op("vector", lambda e: e.tensor_scalar(
                stat[:, 1:2], stat[:, 0:1], 1.0 / D, None, ALU.mult), inc=dve)
            va = P.op("scalar", lambda e, xa=xa: e.activation(
                sq_buf[:], xa, AF.Square, accum_out=stat[:, 2:3]),
                waits=xw + [(dve, dve.n)], inc=act)
            P.op("vector", lambda e: e.tensor_tensor(
                stat[:, 3:4], stat[:, 1:2], stat[:, 1:2], ALU.mult),
                waits=[(act, va)], inc=dve)
            P.op("vector", lambda e: e.tensor_scalar(
                stat[:, 4:5], stat[:, 2:3], 1.0 / D, eps, ALU.mult, ALU.add), inc=dve)
            v2 = P.op("vector", lambda e: e.tensor_tensor(
                stat[:, 4:5], stat[:, 4:5], stat[:, 3:4], ALU.subtract), inc=dve)
            P.op("scalar", lambda e: e.activation(
                stat[:, 5:6], stat[:, 4:5], AF.Ln), waits=[(dve, v2)], inc=act)
            v4 = P.op("scalar", lambda e: e.activation(
                stat[:, 6:7], stat[:, 5:6], AF.Exp, scale=-0.5), inc=act)
            wv_ = [(act, v4)]
            if xn_free[b]:
                wv_.append((pe, xn_free[b]))
            v5 = P.op("vector", lambda e, xa=xa, b=b: e.tensor_scalar(
                xn_bufs[b][:], xa, stat[:, 1:2], stat[:, 6:7], ALU.add, ALU.mult),
                waits=wv_, inc=dve)
            x_free_sink(tt, v5)
            for dc in range(DC):
                hb_ = dc % 2
                wv_ = [(dve, max(v5, ev_done[hb_]))]
                vt = P.op("tensor", lambda e, b=b, dc=dc, hb_=hb_: e.transpose(
                    ps_tr[hb_][:],
                    xn_bufs[b][:, dc * 128:(dc + 1) * 128], ident[:]),
                    waits=wv_, inc=pe)
                tr_done[hb_] = vt
                if dc == DC - 1:
                    xn_free[b] = vt
                vv = P.op("vector", lambda e, tt=tt, dc=dc, hb_=hb_: e.tensor_scalar(
                    out_fm[:, dc * out_stride + tt * 128: dc * out_stride + tt * 128 + 128],
                    ps_tr[hb_][:],
                    gcol[:, dc:dc + 1], scol[:, dc:dc + 1], ALU.mult, ALU.add),
                    waits=[(pe, vt)], inc=dve)
                ev_done[hb_] = vv
            tile_ready[tt] = ev_done[(DC - 1) % 2]
            if after_tile is not None:
                after_tile(tt)

    barrier(["vector"],
            [(pe, max(wm_used)), (S["s_row"], v_rows_done),
             (S["s_mod"], S["s_mod"].n), (S["s_modg"], S["s_modg"].n)])

    h_tile_ready = [0] * NJT
    x_free = [0, 0]

    def x1_src(tt):
        b = tt % 2
        wv_ = [(dve, x_free[b])] if x_free[b] else []
        S_ = S["s_xa" if b == 0 else "s_xb"]
        P.op("sync", dma(xsl[b][:], x_in[tt * 128:(tt + 1) * 128, :]),
             waits=wv_, inc=S_, inc_k=16)
        return xsl[b][:]

    x1_waits = {}

    def x1_ap(tt):
        ap = x1_src(tt)
        x1_waits[tt] = [(S["s_xa" if tt % 2 == 0 else "s_xb"],
                         S["s_xa" if tt % 2 == 0 else "s_xb"].n)]
        return ap

    wv_state = {"cast": 0, "used": 0, "vev": 0}
    wv_pre0 = {}

    def prefetch_wv0(g):
        for half in range(VEW // 256):
            wv_ = [(act, wv_state["cast"])] if wv_state["cast"] else []
            c0 = g * VEW + half * 256
            vd = P.op("sync", dma(
                wvh_f[:].rearrange("p (c n) -> p c n", n=256),
                wv_in[:, c0:c0 + 256].rearrange("(c p) n -> p c n", p=128)),
                waits=wv_, inc=S["s_wv"], inc_k=16)
            cw = [(S["s_wv"], vd)]
            if wv_state["used"]:
                cw.append((pe, wv_state["used"]))
            wv_state["cast"] = P.op("scalar", lambda e, half=half: e.copy(
                wv_sl[:].rearrange("p (c n) -> p c n", n=VEW)[:, :, half * 256:(half + 1) * 256],
                wvh_f[:].rearrange("p (c n) -> p c n", n=256)),
                waits=cw, inc=act)
        wv_pre0[g] = wv_state["cast"]

    prefetch_wv0(0)
    v_evw = [0, 0]
    _tbl = [False]

    def g0_vtile(jt):
        if jt == 2 and not _tbl[0]:
            load_tables()
            _tbl[0] = True
        def vmm(e, jt=jt):
            ins = None
            for dc in range(DC):
                ins = e.matmul(ps_v[jt % 2][:],
                               h_fm[:, dc * TCTX + jt * 128: dc * TCTX + jt * 128 + 128],
                               wv_sl[:, dc * VEW:(dc + 1) * VEW],
                               start=(dc == 0), stop=(dc == DC - 1))
            return ins
        wv2 = [(act, wv_pre0[0]), (dve, h_tile_ready[jt])]
        if v_evw[jt % 2]:
            wv2.append((dve, v_evw[jt % 2]))
        vp = P.op("tensor", vmm, waits=wv2, inc=pe, tag=f"vmm g0 jt{jt}")
        wv_state["used"] = vp
        v_evw[jt % 2] = P.op("vector", lambda e, jt=jt: e.tensor_copy(
            v_sb[:, jt * VEW:(jt + 1) * VEW], ps_v[jt % 2][:]),
            waits=[(pe, vp)], inc=dve)
        wv_state["vev"] = v_evw[jt % 2]

    emit_ln(NJT, g1c, s1c, h_fm, TCTX,
            x1_ap, lambda tt: x1_waits[tt],
            lambda tt, v: x_free.__setitem__(tt % 2, v),
            ps_t, xnsl, sqscr, h_tile_ready, after_tile=g0_vtile)
    P.flush()
    f1.close()
    if int(os.environ.get("DK_PHASES", "5")) <= 2:
        f12.close()
        es.close()
        return nc

    f2 = ExitStack()
    KQO = 0
    SO = 1024
    AVO = 2048
    DENO = AVO + ((TOWN + 511) // 512) * 512
    assert DENO + TOWN <= 4096
    arena = f2.enter_context(nc.psum_tensor("arena", [128, 4096], F32))

    krot = [sbuf(f2, f"krot{i}", [128, TCTX], BF16) for i in range(2)]
    qrot = [sbuf(f2, f"qrot{i}", [128, TOWN], BF16) for i in range(2)]
    pt_sl = [sbuf(f2, f"pt{i}", [128, PTW], BF16) for i in range(2)]
    wq_sl = [sbuf(f2, f"wqs{i}", [128, DC * 128], BF16) for i in range(2)]
    wk_sl = [sbuf(f2, f"wks{i}", [128, DC * 128], BF16) for i in range(2)]
    wqk_f = sbuf(f2, "wqk_f", [128, DC * 128], F32)
    rec_bc = sbuf(f2, "rec_bc", [128, TOWN], F32)
    ao_sl = [sbuf(f2, f"ao{i}", [128, TOWN], BF16) for i in range(2)]
    facc = sbuf(f2, "facc", [128, DC * EGW], F32)

    barrier(["gpsimd"], [(dve, dve.n), (act, act.n)])

    v_fprev = 0
    for eg in range(D // EGW):
        for r in range(R):
            wv_ = [(S["s_fga"], S["s_fga"].n)]
            if r == 0:
                wv_.append((S["s_fso"], S["s_fso"].n))
            vld = P.op("gpsimd", lambda e, r=r, eg=eg: e.dma_start(
                out=facc[:].rearrange("p (c n) -> p c n", n=EGW),
                in_=fac_in[r, :, eg * EGW:(eg + 1) * EGW].rearrange("(c p) n -> p c n", p=128),
                accum_op=(ALU.bypass if r == 0 else ALU.add)),
                waits=wv_, inc=S["s_fga"], inc_k=16, dma=True)
        vst = P.op("gpsimd", dma(
            fsum_d[:, eg * EGW:(eg + 1) * EGW].rearrange("(c p) n -> p c n", p=128),
            facc[:].rearrange("p (c n) -> p c n", n=EGW)),
            waits=[(S["s_fga"], S["s_fga"].n), (S["s_fso"], S["s_fso"].n)],
            inc=S["s_fso"], inc_k=16, dma=True)
        v_fprev = 0
    v_fsum_all = S["s_fso"].n

    wq_used = [0, 0]
    wk_used = [0, 0]
    wv_used = wv_state["used"]
    qk_stage_free = [0]
    qk_cast_done = {}

    def prefetch_qk(h):
        b = h % 2
        wv_ = [(act, qk_stage_free[0])] if qk_stage_free[0] else []
        vq_d = P.op("sync", dma(
            wqk_f[:].rearrange("p (c n) -> p c n", n=128),
            wq_in[:, h * 128:(h + 1) * 128].rearrange("(c p) n -> p c n", p=128)),
            waits=wv_, inc=S["s_wqa"], inc_k=16)
        cw = [(S["s_wqa"], vq_d)]
        if wq_used[b]:
            cw.append((pe, wq_used[b]))
        vq_c = P.op("scalar", lambda e, b=b: e.copy(wq_sl[b][:], wqk_f[:]),
                    waits=cw, inc=act)
        vk_d = P.op("sync", dma(
            wqk_f[:].rearrange("p (c n) -> p c n", n=128),
            wk_in[:, h * 128:(h + 1) * 128].rearrange("(c p) n -> p c n", p=128)),
            waits=[(act, vq_c)], inc=S["s_wqa"], inc_k=16)
        cw = [(S["s_wqa"], vk_d)]
        if wk_used[b]:
            cw.append((pe, wk_used[b]))
        vk_c = P.op("scalar", lambda e, b=b: e.copy(wk_sl[b][:], wqk_f[:]),
                    waits=cw, inc=act)
        qk_stage_free[0] = vk_c
        qk_cast_done[h] = (vq_c, vk_c)

    prefetch_qk(0)
    prefetch_qk(1)
    ao_stored = [0, 0]
    v_sb_ready = 0
    exp_prev_last = 0
    denav_prev_pe = 0
    denav_prev_dve = 0
    pt_consumed = [0, 0]
    rec_read_prev = 0
    den_dma_prev = 0

    sconsumed = [0, 0, 0]

    wv_cast_done = wv_state["cast"]
    wv_prefetched = {}

    def prefetch_wv(g):
        nonlocal wv_cast_done, wv_used
        for half in range(VEW // 256):
            wv_ = [(act, wv_cast_done)] if wv_cast_done else []
            c0 = g * VEW + half * 256
            vd = P.op("sync", dma(
                wvh_f[:].rearrange("p (c n) -> p c n", n=256),
                wv_in[:, c0:c0 + 256].rearrange("(c p) n -> p c n", p=128)),
                waits=wv_, inc=S["s_wv"], inc_k=16)
            cw = [(S["s_wv"], vd)]
            if wv_used:
                cw.append((pe, wv_used))
            wv_cast_done = P.op("scalar", lambda e, half=half: e.copy(
                wv_sl[:].rearrange("p (c n) -> p c n", n=VEW)[:, :, half * 256:(half + 1) * 256],
                wvh_f[:].rearrange("p (c n) -> p c n", n=256)),
                waits=cw, inc=act)
        wv_prefetched[g] = wv_cast_done

    kq_bank_free = [0, 0]
    rope_done_k = {}
    rope_done_q = {}
    rope_all = {}
    smm_last_of = {}

    def emit_kqrope(hh):
        vwq, vwk = qk_cast_done[hh]
        rope_done_k[hh] = {}
        rope_done_q[hh] = {}
        guard = [(pe, smm_last_of[hh - 2])] if (hh - 2) in smm_last_of else []
        cidx = 0
        v = 0
        for kind, src_off, width, castv, out, book, cs_off in (
                ("k", 0, TCTX, vwk, krot[hh % 2], rope_done_k[hh], 0),
                ("q", W, TOWN, vwq, qrot[hh % 2], rope_done_q[hh], W)):
            wsl = wk_sl[hh % 2] if kind == "k" else wq_sl[hh % 2]
            for (n0, n1) in splits(0, width, 512):
                bank = cidx % 2
                cidx += 1
                wid = n1 - n0

                def pmm(e, n0=n0, n1=n1, wsl=wsl, src_off=src_off, bank=bank):
                    ins = None
                    for dc in range(DC):
                        ins = e.matmul(
                            arena[:, KQO + bank * 512:KQO + bank * 512 + (n1 - n0)],
                            wsl[:, dc * 128:(dc + 1) * 128],
                            h_fm[:, dc * TCTX + src_off + n0: dc * TCTX + src_off + n1],
                            start=(dc == 0), stop=(dc == DC - 1))
                    return ins
                wts = [(act, castv), (dve, max(h_tile_ready))]
                if kq_bank_free[bank]:
                    wts.append((dve, kq_bank_free[bank]))
                vmmv = P.op("tensor", pmm, waits=wts, inc=pe,
                            tag=f"kq h{hh} {kind} {n0}")
                if kind == "k":
                    wk_used[hh % 2] = vmmv
                else:
                    wq_used[hh % 2] = vmmv
                pv = arena[:, KQO + bank * 512:KQO + bank * 512 + wid]
                wts2 = [(pe, vmmv), (S["s_const"], v_const2)] + guard
                P.op("vector", lambda e, pv=pv, n0=n0, wid=wid, cs_off=cs_off: e.tensor_tensor(
                    scr_a[:, 0:wid], pv,
                    costab[:, cs_off + n0:cs_off + n0 + wid], ALU.mult), waits=wts2)
                P.op("vector", lambda e, pv=pv, n0=n0, wid=wid, cs_off=cs_off: e.tensor_tensor(
                    scr_b[0:64, 0:wid], pv[64:128, :],
                    sintab[0:64, cs_off + n0:cs_off + n0 + wid], ALU.mult))
                P.op("vector", lambda e, pv=pv, n0=n0, wid=wid, cs_off=cs_off: e.tensor_tensor(
                    scr_b[64:128, 0:wid], pv[0:64, :],
                    sintab[64:128, cs_off + n0:cs_off + n0 + wid], ALU.mult))
                v = P.op("vector", lambda e, out=out, n0=n0, wid=wid: e.tensor_tensor(
                    out[:, n0:n0 + wid], scr_a[:, 0:wid], scr_b[:, 0:wid], ALU.add),
                    tag=f"rope h{hh} {kind} {n0}")
                book[n0] = v
                kq_bank_free[bank] = v
        rope_all[hh] = v

    emit_kqrope(0)

    for h in range(H):
        g, hg = divmod(h, HG)
        if hg == 0 and g == 0:
            v_sb_ready = wv_state["vev"]
        if hg == 0 and g > 0:
            vwv = wv_prefetched[g]
            v_ev = 0
            for jt in range(NJT):
                sb_ = jt % 2

                def vmm(e, jt=jt, sb_=sb_):
                    ins = None
                    for dc in range(DC):
                        ins = e.matmul(arena[:, SO + sb_ * 512:SO + sb_ * 512 + VEW],
                                       h_fm[:, dc * TCTX + jt * 128: dc * TCTX + jt * 128 + 128],
                                       wv_sl[:, dc * VEW:(dc + 1) * VEW],
                                       start=(dc == 0), stop=(dc == DC - 1))
                    return ins
                wv2 = [(act, vwv), (dve, h_tile_ready[jt]),
                       (act, max(sconsumed[sb_], exp_prev_last if jt < 2 else 0))]
                if jt >= 2:
                    wv2.append((dve, v_ev))
                vp = P.op("tensor", vmm, waits=wv2, inc=pe, tag=f"vmm g{g} jt{jt}")
                wv_used = vp
                v_ev = P.op("vector", lambda e, jt=jt, sb_=sb_: e.tensor_copy(
                    v_sb[:, jt * VEW:(jt + 1) * VEW], arena[:, SO + sb_ * 512:SO + sb_ * 512 + VEW]),
                    waits=[(pe, vp)], inc=dve)
            v_sb_ready = v_ev

        if h + 1 < H:
            emit_kqrope(h + 1)

        si = 0
        den_started = [False] * max(1, TOWN // 512)
        av_started = [False] * max(1, TOWN // 512)
        last_pe = 0
        pending = None

        _order = [W // 128] + [j for j in range(NJT) if j != W // 128]
        _chunk_seq = []
        for _jt in _order:
            _j0 = _jt * 128
            _A0 = max(0, _j0 - W)
            _A1 = min(TOWN, _j0 + 128)
            for (_a0, _a1) in splits(_A0, _A1, 512):
                _chunk_seq.append((_jt, _a0, _a1))
        _last_of_bank = {}
        for _i, (_jt, _a0, _a1) in enumerate(_chunk_seq):
            _last_of_bank[_a0 // 512] = _i
        _seq_idx = [0]

        def emit_denav(jt, A0, chunks, v_exp, v_tri, pos=None, h=h, hg=hg):
            nonlocal last_pe
            ptl = pt_sl[pos % 2]
            for (a0, a1) in chunks:
                di = a0 // 512
                idx = _seq_idx[0]
                _seq_idx[0] += 1
                assert _chunk_seq[idx] == (jt, a0, a1)
                isl = (_last_of_bank[di] == idx)

                def dmm(e, a0=a0, a1=a1, ptl=ptl, A0=A0,
                        st=(not den_started[di]), sp=isl):
                    return e.matmul(arena[:, DENO + a0:DENO + a1],
                                    ones_c[:], ptl[:, a0 - A0:a1 - A0],
                                    start=st, stop=sp)
                wts = [(act, v_exp), (dve, v_tri)]
                if h > 0 and not any(den_started) and not any(av_started):
                    wts.append((dve, denav_prev_dve))
                P.op("tensor", dmm, waits=wts, inc=pe, tag=f"den h{h} jt{jt} q{a0}")
                den_started[di] = True

                def amm(e, jt=jt, A0=A0, a0=a0, a1=a1, hg=hg, ptl=ptl,
                        st=(not av_started[di]), sp=isl):
                    return e.matmul(arena[:, AVO + a0:AVO + a1],
                                    v_sb[:, jt * VEW + hg * 128: jt * VEW + hg * 128 + 128],
                                    ptl[:, a0 - A0:a1 - A0], start=st, stop=sp)
                wts2 = [(dve, v_sb_ready), (act, v_exp), (dve, v_tri)]
                if h > 0 and not any(av_started):
                    wts2.append((dve, denav_prev_dve))
                last_pe = P.op("tensor", amm, waits=wts2, inc=pe,
                               tag=f"av h{h} jt{jt} q{a0}")
                av_started[di] = True

        for pos, jt in enumerate(_order):
            j0 = jt * 128
            A0 = max(0, j0 - W)
            A1 = min(TOWN, j0 + 128)
            if pos == 0:
                assert (A0, A1) == (0, TOWN)
            chunks = splits(A0, A1, 512)
            ptl = pt_sl[pos % 2]
            s_mm = []
            for (a0, a1) in chunks:
                slx = si % 2
                si += 1

                def smm(e, jt=jt, a0=a0, a1=a1, slx=slx, h=h):
                    return e.matmul(arena[:, SO + slx * 512:SO + slx * 512 + (a1 - a0)],
                                    krot[h % 2][:, jt * 128:jt * 128 + 128],
                                    qrot[h % 2][:, a0:a1], start=True, stop=True)
                wts = [(dve, rope_done_k[h][(jt * 128 // 512) * 512]),
                       (dve, rope_done_q[h][(a0 // 512) * 512])]
                if sconsumed[slx]:
                    wts.append((act, sconsumed[slx]))
                if hg == 0:
                    wts.append((dve, v_sb_ready))
                vs = P.op("tensor", smm, waits=wts, inc=pe,
                          tag=f"smm h{h} jt{jt} a{a0}")
                s_mm.append((a0, a1, vs, slx))
            smm_last = s_mm[-1][2]
            vexp = 0
            for (a0, a1, vs, slx) in s_mm:
                def expf(e, jt=jt, a0=a0, a1=a1, slx=slx, A0=A0, ptl=ptl):
                    return e.activation(ptl[:, a0 - A0:a1 - A0],
                                        arena[:, SO + slx * 512:SO + slx * 512 + (a1 - a0)],
                                        AF.Exp, bias=halo_b[:, jt:jt + 1])
                wts = [(pe, vs)]
                if pt_consumed[pos % 2]:
                    wts.append((pe, pt_consumed[pos % 2]))
                vexp = P.op("scalar", expf, waits=wts, inc=act,
                            tag=f"exp h{h} jt{jt} a{a0}")
                sconsumed[slx] = vexp
            tris = []
            if j0 < TOWN:
                tris.append((j0 - A0, tri_lo))
            if j0 >= W:
                tris.append((0, tri_hi))
            vt_ = 0
            for toff, tmask in tris:
                vt_ = P.op("vector",
                           lambda e, toff=toff, tmask=tmask, ptl=ptl: e.tensor_tensor(
                               ptl[:, toff:toff + 128], ptl[:, toff:toff + 128],
                               tmask[:], ALU.mult),
                           waits=[(act, vexp)], inc=dve, tag=f"tri h{h} jt{jt}")
            if pending is not None:
                emit_denav(*pending[1:], pos=pending[0])
                pt_consumed[pending[0] % 2] = last_pe
            pending = (pos, jt, A0, chunks, vexp, vt_)
        smm_last_of[h] = smm_last
        if h + 2 < H:
            prefetch_qk(h + 2)
        if hg == 2 and g + 1 < H // HG:
            prefetch_wv(g + 1)
        emit_denav(*pending[1:], pos=pending[0])
        pt_consumed[pending[0] % 2] = last_pe
        pending = None
        exp_prev_last = sconsumed[(si - 1) % 2]

        wts = [(pe, last_pe)]
        if rec_read_prev:
            wts.append((dve, rec_read_prev))
        vdc = P.op("vector", lambda e: e.tensor_copy(
            rec_bc[:], arena[:, DENO:DENO + TOWN]), waits=wts, inc=dve)
        vdb = P.op("vector", lambda e: e.reciprocal(rec_bc[:], rec_bc[:]),
                   waits=[(dve, vdc)], inc=dve)

        b = h % 2
        wts = [(pe, last_pe), (dve, vdb)]
        if ao_stored[b]:
            wts.append((S["s_aoa" if b == 0 else "s_aob"], ao_stored[b]))
        vev = P.op("vector", lambda e, b=b: e.scalar_tensor_tensor(
            ao_sl[b][:], arena[:, AVO:AVO + TOWN], 1.0, rec_bc[:], ALU.mult, ALU.mult),
            waits=wts, inc=dve)
        rec_read_prev = vev
        denav_prev_dve = vev
        ao_stored[b] = P.op("sync", dma(ao_d[h * 128:(h + 1) * 128, :], ao_sl[b][:]),
                            waits=[(dve, vev)],
                            inc=S["s_aoa" if b == 0 else "s_aob"], inc_k=16)

    v_ao_all = (S["s_aoa"].n, S["s_aob"].n)
    P.flush()
    f2.close()
    f12.close()
    if int(os.environ.get("DK_PHASES", "5")) <= 3:
        es.close()
        return nc

    f34 = ExitStack()
    x1_sb = sbuf(f34, "x1_sb", [128, NQT * D], F32)
    h2_fm = sbuf(f34, "h2_fm", [128, DC * TOWN], BF16)
    gb_b = sbuf(f34, "gb_b", [128, D], BF16)
    gbb_b = sbuf(f34, "gbb_b", [128, D], BF16)
    f3 = ExitStack()
    ps_t3 = [f3.enter_context(nc.psum_tensor(f"ps_t3{i}", [128, 128], BF16)) for i in range(2)]
    ps_wo = f3.enter_context(nc.psum_tensor("ps_wo", [128, 1024], F32))
    ao_sb = sbuf(f3, "ao_sb", [128, DC * TOWN], BF16)
    wo_sl = [sbuf(f3, f"wo_sl{i}", [128, DC * 512], BF16) for i in range(2)]
    wo_f = sbuf(f3, "wo_f", [128, DC * 256], F32)
    ga_b = sbuf(f3, "ga_b", [128, D], BF16)
    xsl3 = [sbuf(f3, f"x3_{i}", [128, 512], F32) for i in range(2)]
    xnsl3 = [sbuf(f3, f"xn3_{i}", [128, D], BF16) for i in range(2)]
    sqscr3 = sbuf(f3, "sqscr3", [128, D], BF16)
    t1scr = sbuf(f3, "t1scr", [128, 512], F32)

    barrier(["sync", "gpsimd", "tensor", "vector", "scalar"],
            [(pe, pe.n), (dve, dve.n), (act, act.n), (gp, gp.n),
             (S["s_fso"], v_fsum_all), (S["s_aoa"], v_ao_all[0]),
             (S["s_aob"], v_ao_all[1]),
             (S["s_den"], S["s_den"].n), (S["s_denb"], S["s_denb"].n)])

    P.op("gpsimd", dma(ga_b[:], rows_d[0:1, :].broadcast_to([128, D])),
         waits=[(S["s_row"], v_rows_done)], inc=S["s_rowg"], inc_k=16, dma=True)
    P.op("gpsimd", dma(gb_b[:], rows_d[1:2, :].broadcast_to([128, D])),
         inc=S["s_rowg"], inc_k=16, dma=True)
    P.op("gpsimd", dma(gbb_b[:], rows_d[2:3, :].broadcast_to([128, D])),
         inc=S["s_rowg"], inc_k=16, dma=True)
    vgb_b = vgbb_b = None
    v_aor = P.op("sync", dma(ao_sb[:].rearrange("p (c n) -> p c n", n=TOWN),
                             ao_d.ap().rearrange("(c p) n -> p c n", p=128)),
                 waits=[(S["s_aoa"], v_ao_all[0]), (S["s_aob"], v_ao_all[1])],
                 inc=S["s_aor"], inc_k=16)

    NDG = D // 512
    wo_used = 0
    x1_ready = [0] * NQT
    h2_tile_ready = [0] * NQT
    _ln2_state = dict(tr=[0, 0], ev=[0, 0], xf=[0, 0])

    def emit_ln2_tile(tt):
        b = tt % 2
        xa = x1_sb[:, tt * D:(tt + 1) * D]
        xw = [(dve, x1_ready[tt])]
        P.op("vector", lambda e, xa=xa: e.reduce_sum(
            stat[:, 0:1], xa, axis=AX, negate=True), waits=xw, inc=dve)
        P.op("vector", lambda e: e.tensor_scalar(
            stat[:, 1:2], stat[:, 0:1], 1.0 / D, None, ALU.mult), inc=dve)
        va = P.op("scalar", lambda e, xa=xa: e.activation(
            sqscr3[:], xa, AF.Square, accum_out=stat[:, 2:3]),
            waits=xw + [(dve, dve.n)], inc=act)
        P.op("vector", lambda e: e.tensor_tensor(
            stat[:, 3:4], stat[:, 1:2], stat[:, 1:2], ALU.mult),
            waits=[(act, va)], inc=dve)
        P.op("vector", lambda e: e.tensor_scalar(
            stat[:, 4:5], stat[:, 2:3], 1.0 / D, eps, ALU.mult, ALU.add), inc=dve)
        v2 = P.op("vector", lambda e: e.tensor_tensor(
            stat[:, 4:5], stat[:, 4:5], stat[:, 3:4], ALU.subtract), inc=dve)
        P.op("scalar", lambda e: e.activation(
            stat[:, 5:6], stat[:, 4:5], AF.Ln), waits=[(dve, v2)], inc=act)
        v4 = P.op("scalar", lambda e: e.activation(
            stat[:, 6:7], stat[:, 5:6], AF.Exp, scale=-0.5), inc=act)
        wv_ = [(act, v4)]
        if _ln2_state["xf"][b]:
            wv_.append((pe, _ln2_state["xf"][b]))
        v5 = P.op("vector", lambda e, xa=xa, b=b: e.tensor_scalar(
            xnsl3[b][:], xa, stat[:, 1:2], stat[:, 6:7], ALU.add, ALU.mult),
            waits=wv_, inc=dve)
        for dc in range(DC):
            hb_ = dc % 2
            wv_ = [(dve, max(v5, _ln2_state["ev"][hb_]))]
            vt = P.op("tensor", lambda e, b=b, dc=dc, hb_=hb_: e.transpose(
                ps_t3[hb_][:],
                xnsl3[b][:, dc * 128:(dc + 1) * 128], ident[:]),
                waits=wv_, inc=pe)
            _ln2_state["tr"][hb_] = vt
            if dc == DC - 1:
                _ln2_state["xf"][b] = vt
            vv = P.op("vector", lambda e, tt=tt, dc=dc, hb_=hb_: e.tensor_scalar(
                h2_fm[:, dc * TOWN + tt * 128: dc * TOWN + tt * 128 + 128],
                ps_t3[hb_][:],
                g2c[:, dc:dc + 1], s2c[:, dc:dc + 1], ALU.mult, ALU.add),
                waits=[(pe, vt)], inc=dve)
            _ln2_state["ev"][hb_] = vv
        h2_tile_ready[tt] = _ln2_state["ev"][(DC - 1) % 2]
    x3_done = [0, 0]
    x3_loaded = [0, 0]
    wo_evac = [0, 0]
    wo_cast_done = 0
    wo_cast_of = {}
    wo_mm_of = {}

    def prefetch_wo(dg):
        nonlocal wo_cast_done
        for half in range(2):
            wv_ = [(act, wo_cast_done)] if wo_cast_done else []
            c0 = dg * 512 + half * 256
            vd = P.op("sync", dma(
                wo_f[:].rearrange("p (c n) -> p c n", n=256),
                wo_in[:, c0:c0 + 256].rearrange("(c p) n -> p c n", p=128)),
                waits=wv_, inc=S["s_wo"], inc_k=16)
            cw = [(S["s_wo"], vd)]
            if dg - 2 in wo_mm_of:
                cw.append((pe, wo_mm_of[dg - 2]))
            wo_cast_done = P.op("scalar", lambda e, half=half, dg=dg: e.copy(
                wo_sl[dg % 2][:].rearrange("p (c n) -> p c n", n=512)[:, :, half * 256:(half + 1) * 256],
                wo_f[:].rearrange("p (c n) -> p c n", n=256)),
                waits=cw, inc=act)
        wo_cast_of[dg] = wo_cast_done

    prefetch_wo(0)
    for dg in range(NDG):
        if dg + 1 < NDG:
            prefetch_wo(dg + 1)
        vwo = wo_cast_of[dg]
        for tt in range(NQT):
            bb = tt % 2
            wv_ = [(dve, x3_done[bb])] if x3_done[bb] else []
            v_x3 = P.op(
                "sync", dma(xsl3[bb][:],
                            x_in[W + tt * 128:W + (tt + 1) * 128, dg * 512:(dg + 1) * 512]),
                waits=wv_, inc=S["s_xoa" if bb == 0 else "s_xob"], inc_k=16)

            def womm(e, tt=tt, dg=dg):
                ins = None
                for ec in range(DC):
                    ins = e.matmul(ps_wo[:, (tt % 2) * 512:(tt % 2) * 512 + 512],
                                   ao_sb[:, ec * TOWN + tt * 128: ec * TOWN + tt * 128 + 128],
                                   wo_sl[dg % 2][:, ec * 512:(ec + 1) * 512],
                                   start=(ec == 0), stop=(ec == DC - 1))
                return ins
            wts = [(act, vwo), (S["s_aor"], v_aor)]
            if wo_evac[tt % 2]:
                wts.append((dve, wo_evac[tt % 2]))
            vwomm = P.op("tensor", womm, waits=wts, inc=pe)
            wo_used = vwomm
            wo_mm_of[dg] = vwomm

            wts = [(pe, vwomm), (S["s_xoa" if bb == 0 else "s_xob"], v_x3),
                   (S["s_rowg"], S["s_rowg"].n)]
            P.op("vector", lambda e, tt=tt, dg=dg: e.scalar_tensor_tensor(
                t1scr[:], ps_wo[:, (tt % 2) * 512:(tt % 2) * 512 + 512],
                1.0, ga_b[:, dg * 512:(dg + 1) * 512], ALU.mult, ALU.mult),
                waits=wts)
            wo_evac[tt % 2] = P.op("vector", lambda e, tt=tt, dg=dg: e.tensor_tensor(
                x1_sb[:, tt * D + dg * 512: tt * D + (dg + 1) * 512],
                t1scr[:], xsl3[tt % 2][:], ALU.add))
            x3_done[bb] = wo_evac[tt % 2]
            if dg == NDG - 1:
                x1_ready[tt] = wo_evac[tt % 2]
                emit_ln2_tile(tt)

    P.flush()
    f3.close()
    if int(os.environ.get("DK_PHASES", "5")) <= 4:
        f34.close()
        es.close()
        return nc

    f4 = ExitStack()
    NEGF = D // 512
    ps_ff = f4.enter_context(nc.psum_tensor("ps_ff", [128, 512 * NEGF], F32))
    f_sb = sbuf(f4, "f_sb", [128, DC * D], BF16)
    out_sl = [sbuf(f4, f"out_sl{i}", [128, D], F32) for i in range(2)]
    t1scr4 = sbuf(f4, "t1scr4", [128, 512], F32)

    barrier(["sync", "tensor", "vector", "scalar"],
            [(pe, pe.n), (dve, dve.n), (act, act.n),
             (S["s_xoa"], S["s_xoa"].n), (S["s_xob"], S["s_xob"].n),
             (S["s_aor"], S["s_aor"].n)])

    for sl4 in range(4):
        c0 = sl4 * (D // 4)
        P.op("sync", dma(
            f_sb[:].rearrange("p (c n) -> p c n", n=D)[:, :, c0:c0 + D // 4],
            fsum_d.ap()[:, c0:c0 + D // 4].rearrange("(c p) n -> p c n", p=128)),
            waits=[(S["s_fso"], v_fsum_all)], inc=S["s_fr"], inc_k=16)
    v_fr = S["s_fr"].n

    x1p_ready = [0] * NQT
    for tt in range(NQT):
        x1p_ready[tt] = P.op("vector", lambda e, tt=tt: e.tensor_tensor(
            x1_sb[:, tt * D:(tt + 1) * D], x1_sb[:, tt * D:(tt + 1) * D],
            gbb_b[:], ALU.add),
            waits=[(dve, h2_tile_ready[tt]), (S["s_rowg"], S["s_rowg"].n)], inc=dve)

    ff_evac = 0
    out_dma_of = [0, 0]
    for tt in range(NQT):
        def ffmm(e, tt=tt):
            ins = None
            for dc in range(DC):
                for eg in range(NEGF):
                    ins = e.matmul(ps_ff[:, eg * 512:(eg + 1) * 512],
                                   h2_fm[:, dc * TOWN + tt * 128: dc * TOWN + tt * 128 + 128],
                                   f_sb[:, dc * D + eg * 512:dc * D + (eg + 1) * 512],
                                   start=(dc == 0), stop=(dc == DC - 1))
            return ins
        wts = [(S["s_fr"], v_fr), (dve, h2_tile_ready[tt])]
        if ff_evac:
            wts.append((dve, ff_evac))
        vf = P.op("tensor", ffmm, waits=wts, inc=pe)

        last_ev = 0
        ob = tt % 2
        for eg in range(NEGF):
            wts = [(pe, vf), (dve, x1p_ready[tt]), (S["s_rowg"], S["s_rowg"].n)]
            if eg == 0 and out_dma_of[ob]:
                wts.append((S["s_ya" if ob == 0 else "s_yb"], out_dma_of[ob]))
            P.op("vector", lambda e, eg=eg: e.scalar_tensor_tensor(
                t1scr4[:], ps_ff[:, eg * 512:(eg + 1) * 512],
                1.0 / math.sqrt(R), gb_b[:, eg * 512:(eg + 1) * 512],
                ALU.mult, ALU.mult),
                waits=wts)
            last_ev = P.op("vector", lambda e, tt=tt, eg=eg, ob=ob: e.tensor_tensor(
                out_sl[ob][:, eg * 512:(eg + 1) * 512], t1scr4[:],
                x1_sb[:, tt * D + eg * 512: tt * D + (eg + 1) * 512], ALU.add))
        ff_evac = last_ev
        out_dma_of[ob] = P.op("sync", dma(y_out[tt * 128:(tt + 1) * 128, :], out_sl[ob][:]),
                              waits=[(dve, last_ev)],
                              inc=S["s_ya" if ob == 0 else "s_yb"], inc_k=16)

    P.op("sync", lambda e: None,
         waits=[(S["s_ya"], S["s_ya"].n), (S["s_yb"], S["s_yb"].n)])
    P.flush()
    f4.close()
    f34.close()
    es.close()
    return nc


CFG = make_cfg()


def shard_inputs(cfg, full):
    D, TOWN, W, TCTX = cfg["D"], cfg["TOWN"], cfg["W"], cfg["TCTX"]
    n = cfg["n_cores"]
    x = np.ascontiguousarray(np.asarray(full["x"], dtype=np.float32)).reshape(-1, D)
    maps = []
    for i in range(n):
        t0 = i * TOWN - W
        if t0 < 0:
            xs = np.concatenate([np.zeros((-t0, D), np.float32), x[0:TCTX + t0]], 0)
        else:
            xs = x[t0:t0 + TCTX]
        tabs = host_tables(cfg, i)
        maps.append(dict(
            x=np.ascontiguousarray(xs),
            wq=full["wq"], wk=full["wk"], wv=full["wv"], wo=full["wo"],
            factors=full["factors"], fbias=full["fbias"].reshape(1, D),
            cvec=full["cvec"].reshape(1, D),
            wmod=np.ascontiguousarray(full["wmod"][:, i * cfg["MODSH"]:(i + 1) * cfg["MODSH"]]),
            costab=tabs["costab"], sintab=tabs["sintab"], halo_bias=tabs["halo_bias"],
            tri_lo=tabs["tri_lo"], tri_hi=tabs["tri_hi"], ident=tabs["ident"],
            identf=tabs["identf"], ones_col=tabs["ones_col"]))
    return maps


def run(cfg, full):
    maps = shard_inputs(cfg, full)
    nc = build_program(cfg)
    res = run_bass_kernel_spmd(nc, maps, core_ids=list(range(cfg["n_cores"])))
    y = np.concatenate([np.asarray(r["y"]) for r in res.results], axis=0)
    return y.reshape(1, cfg["n_cores"] * cfg["TOWN"], cfg["D"]).astype(np.float32)


def kernel(L0_attn_hidden_states, L0_attn_wq, L0_attn_wk, L0_attn_wv,
           L0_attn_wo, L0_ffn_factors, L0_ffn_bias, L0_c, L0_wmod):
    full = dict(x=L0_attn_hidden_states,
                wq=np.asarray(L0_attn_wq, np.float32),
                wk=np.asarray(L0_attn_wk, np.float32),
                wv=np.asarray(L0_attn_wv, np.float32),
                wo=np.asarray(L0_attn_wo, np.float32),
                factors=np.asarray(L0_ffn_factors, np.float32),
                fbias=np.asarray(L0_ffn_bias, np.float32),
                cvec=np.asarray(L0_c, np.float32),
                wmod=np.asarray(L0_wmod, np.float32))
    return run(CFG, full)



# version 25
# speedup vs baseline: 1.0523x; 1.0523x over previous
import math
import os
import numpy as np
from contextlib import ExitStack

import concourse.bass as bass
import concourse.mybir as mybir
from concourse.bass_utils import run_bass_kernel_spmd
import ml_dtypes

BF = ml_dtypes.bfloat16
F32 = mybir.dt.float32
BF16 = mybir.dt.bfloat16
AF = mybir.ActivationFunctionType
ALU = mybir.AluOpType
AX = mybir.AxisListType.X

NEG_BIG = -1.0e9


def make_cfg(D=2048, TOWN=1024, W=1024, R=8, n_cores=8):
    c = dict(D=D, TOWN=TOWN, W=W, R=R, n_cores=n_cores)
    c["HD"] = 128
    c["H"] = D // 128
    c["DC"] = D // 128
    c["TCTX"] = TOWN + W
    c["NJT"] = c["TCTX"] // 128
    c["NQT"] = TOWN // 128
    c["MODW"] = 6 * D
    c["MODSH"] = c["MODW"] // n_cores
    c["theta"] = 10000.0
    c["eps"] = 1e-5
    c["EGW"] = 256
    c["HG"] = min(4, c["H"])
    return c


def splits(off, end, step):
    out = []
    a = off
    while a < end:
        b = min(end, (a // step + 1) * step)
        out.append((a, b))
        a = b
    return out


def host_tables(cfg, core):
    TOWN, W, TCTX = cfg["TOWN"], cfg["W"], cfg["TCTX"]
    HD = cfg["HD"]
    t0 = TOWN * core - W
    pos = np.arange(TCTX, dtype=np.float64) + t0
    inv = 1.0 / (cfg["theta"] ** (np.arange(0, HD, 2, dtype=np.float64) / HD))
    fr = pos[:, None] * inv[None, :]
    s4 = float(HD) ** -0.25
    cos = (np.cos(fr) * s4).astype(np.float32).T
    sin = (np.sin(fr) * s4).astype(np.float32).T
    costab = np.ascontiguousarray(np.concatenate([cos, cos], 0))
    sintab = np.ascontiguousarray(np.concatenate([-sin, sin], 0))
    hb = np.zeros((128, cfg["NJT"]), dtype=np.float32)
    if core == 0:
        hb[:, : W // 128] = NEG_BIG
    p = np.arange(128)
    tri_lo = (p[None, :] < p[:, None]).astype(BF)
    tri_hi = (p[None, :] >= p[:, None]).astype(BF)
    return dict(costab=costab, sintab=sintab, halo_bias=hb, tri_lo=tri_lo,
                tri_hi=tri_hi, ident=np.eye(128, dtype=BF),
                identf=np.eye(128, dtype=np.float32),
                ones_col=np.ones((128, 128), dtype=BF))


class SemT:
    def __init__(self, nc, es, name):
        self.h = es.enter_context(nc.semaphore(name))
        self.n = 0


class Prog:
    ENGS = ("tensor", "vector", "scalar", "gpsimd", "sync")
    TAGS = []
    WSEEN = {}
    AUTO = {"vector": "dve", "scalar": "act", "gpsimd": "gp"}

    def __init__(self, nc, esem):
        self.nc = nc
        self.ops = []
        self.esem = esem
        Prog.WSEEN = {}
        Prog.TAGS = []

    def op(self, eng, fn, waits=(), inc=None, inc_k=1, dma=False, tag=""):
        w = [(s, v) for (s, v) in waits if v > 0]
        if eng in self.AUTO:
            sem = self.esem[self.AUTO[eng]]
            if sem.n > 0:
                w.append((sem, sem.n))
            if inc is None and not dma:
                inc, inc_k = sem, 1
        self.ops.append((eng, fn, w, inc, inc_k, tag))
        if inc is not None:
            inc.n += inc_k
            return inc.n
        return None

    def flush(self):
        byeng = {e: [] for e in self.ENGS}
        for o in self.ops:
            byeng[o[0]].append(o)
        with self.nc.Block() as block:
            for ename in self.ENGS:
                ops = byeng[ename]
                if not ops:
                    continue

                def emit(e, ops=ops, ename=ename):
                    seen = Prog.WSEEN.setdefault(ename, {})
                    for (_, fn, waits, inc, inc_k, tag) in ops:
                        n0 = self.nc.n_instructions()
                        for (s, v) in waits:
                            if seen.get(id(s), -1) >= v:
                                continue
                            seen[id(s)] = v
                            e.wait_ge(s.h, v)
                        ins = fn(e)
                        if inc is not None:
                            assert ins is not None, "op with inc returned None"
                            ins.then_inc(inc.h, inc_k)
                        if tag:
                            n1 = self.nc.n_instructions()
                            Prog.TAGS.append((n0, n1, tag))
                getattr(block, ename)(emit)
        self.ops = []


def build_program(cfg):
    D, TOWN, W, TCTX, R = cfg["D"], cfg["TOWN"], cfg["W"], cfg["TCTX"], cfg["R"]
    H, DC = cfg["H"], cfg["DC"]
    NJT, NQT = cfg["NJT"], cfg["NQT"]
    MODSH = cfg["MODSH"]
    EGW, HG = cfg["EGW"], cfg["HG"]
    PTW = min(TOWN, W + 128)
    VEW = HG * 128
    eps = cfg["eps"]

    nc = bass.Bass("TRN2", target_bir_lowering=False)
    es = ExitStack()

    x_in = nc.dram_tensor("x", [TCTX, D], F32, kind="ExternalInput").ap()
    wq_in = nc.dram_tensor("wq", [D, D], F32, kind="ExternalInput").ap()
    wk_in = nc.dram_tensor("wk", [D, D], F32, kind="ExternalInput").ap()
    wv_in = nc.dram_tensor("wv", [D, D], F32, kind="ExternalInput").ap()
    wo_in = nc.dram_tensor("wo", [D, D], F32, kind="ExternalInput").ap()
    fac_in = nc.dram_tensor("factors", [R, D, D], F32, kind="ExternalInput").ap()
    bias_in = nc.dram_tensor("fbias", [1, D], F32, kind="ExternalInput").ap()
    c_in = nc.dram_tensor("cvec", [1, D], F32, kind="ExternalInput").ap()
    wmod_in = nc.dram_tensor("wmod", [D, MODSH], F32, kind="ExternalInput").ap()
    cos_in = nc.dram_tensor("costab", [128, TCTX], F32, kind="ExternalInput").ap()
    sin_in = nc.dram_tensor("sintab", [128, TCTX], F32, kind="ExternalInput").ap()
    hb_in = nc.dram_tensor("halo_bias", [128, NJT], F32, kind="ExternalInput").ap()
    trilo_in = nc.dram_tensor("tri_lo", [128, 128], BF16, kind="ExternalInput").ap()
    trihi_in = nc.dram_tensor("tri_hi", [128, 128], BF16, kind="ExternalInput").ap()
    ident_in = nc.dram_tensor("ident", [128, 128], BF16, kind="ExternalInput").ap()
    identf_in = nc.dram_tensor("identf", [128, 128], F32, kind="ExternalInput").ap()
    ones_in = nc.dram_tensor("ones_col", [128, 128], BF16, kind="ExternalInput").ap()
    y_out = nc.dram_tensor("y", [TOWN, D], F32, kind="ExternalOutput").ap()

    modl_d = nc.dram_tensor("modl_d", [1, MODSH], F32)
    modg_d = nc.dram_tensor("modg_d", [cfg["n_cores"], MODSH], F32)
    rows_d = nc.dram_tensor("rows_d", [4, D], F32)
    dens_d = nc.dram_tensor("dens_d", [H, TOWN], F32)
    ao_d = nc.dram_tensor("ao_d", [D, TOWN], BF16)
    fsum_d = nc.dram_tensor("fsum_d", [D, D], BF16)
    mod_flat = modg_d.ap().rearrange("a b -> (a b)")

    sem_names = [
        "pe", "dve", "act", "gp",
        "s_const", "s_xa", "s_xb", "s_wma", "s_wmb",
        "s_wqa", "s_wqb", "s_wka", "s_wkb", "s_wv",
        "s_fga", "s_fgb", "s_fso", "s_mod", "s_den", "s_denb", "s_aoa", "s_aob", "s_aor",
        "s_wo", "s_xoa", "s_xob", "s_fr", "s_ya", "s_yb", "s_row", "s_rowg", "s_cc", "s_modg",
    ]
    S = {n: SemT(nc, es, n) for n in sem_names}
    pe, dve, act, gp = S["pe"], S["dve"], S["act"], S["gp"]

    def sbuf(stack, name, shape, dt):
        return stack.enter_context(nc.sbuf_tensor(name, shape, dt))

    ident = sbuf(es, "identb", [128, 128], BF16)
    identf = sbuf(es, "identf_s", [128, 128], F32)
    g1c = sbuf(es, "g1c", [128, DC], F32)
    s1c = sbuf(es, "s1c", [128, DC], F32)
    g2c = sbuf(es, "g2c", [128, DC], F32)
    s2c = sbuf(es, "s2c", [128, DC], F32)
    siluc = sbuf(es, "siluc", [128, DC], F32)
    stat = sbuf(es, "stat", [128, 10], F32)

    P = Prog(nc, {"dve": S["dve"], "act": S["act"], "gp": S["gp"]})

    def dma(dst, src):
        return lambda e: e.dma_start(out=dst, in_=src)

    def barrier(engines, waits):
        for eng in engines:
            P.op(eng, lambda e: None, waits=list(waits), dma=True)

    f0 = ExitStack()
    SW = min(512, MODSH)
    NSL = (MODSH + SW - 1) // SW
    ps_mv = f0.enter_context(nc.psum_tensor("ps_mv", [1, 1024], F32))
    ps_tf = f0.enter_context(nc.psum_tensor("ps_tf", [128, DC], F32))
    wm_sl = [sbuf(f0, f"wm{i}", [128, DC * SW], F32) for i in range(2)]
    rowbuf = sbuf(f0, "rowbuf", [1, max(D, MODSH)], F32)
    rowbuf2 = sbuf(f0, "rowbuf2", [1, D], F32)
    ctile = sbuf(f0, "ctile", [DC, 128], F32)
    mslice = sbuf(f0, "mslice", [DC, 128], F32)
    scrr_a = sbuf(f0, "scrr_a", [1, D], F32)
    scrr_b = sbuf(f0, "scrr_b", [1, D], F32)

    for dst, srcap in [
            (ident[:], ident_in), (identf[:], identf_in),
            (ctile[:], c_in.rearrange("u (a p) -> (u a) p", p=128))]:
        P.op("sync", dma(dst, srcap), inc=S["s_const"], inc_k=16)
    v_const = S["s_const"].n

    wm_used = [0, 0]

    v_silu = P.op("scalar", lambda e: e.activation(ctile[:], ctile[:], AF.Silu),
                  waits=[(S["s_const"], v_const)], inc=act)
    v_ct = P.op("tensor", lambda e: e.transpose(ps_tf[:], ctile[:], identf[0:DC, 0:DC]),
                waits=[(act, v_silu), (S["s_const"], v_const)], inc=pe)
    v_sc = P.op("vector", lambda e: e.tensor_copy(siluc[:], ps_tf[:]),
                waits=[(pe, v_ct)], inc=dve)

    prev_evac = 0
    for sl in range(NSL):
        b = sl % 2
        sw0, sw1 = sl * SW, min(MODSH, (sl + 1) * SW)
        wid = sw1 - sw0
        wv_ = [(pe, wm_used[b])] if wm_used[b] else []
        vsl = P.op("sync", dma(
            wm_sl[b][:, 0:DC * wid].rearrange("p (c n) -> p c n", n=wid),
            wmod_in[:, sw0:sw1].rearrange("(c p) n -> p c n", p=128)),
            waits=wv_, inc=S["s_wma" if b == 0 else "s_wmb"], inc_k=16)

        def mv(e, b=b, wid=wid):
            ins = None
            pb = ps_mv[0:1, (b % 2) * 512:(b % 2) * 512 + wid]
            for dc in range(DC):
                ins = e.matmul(pb, siluc[:, dc:dc + 1],
                               wm_sl[b][:, dc * wid:(dc + 1) * wid],
                               start=(dc == 0), stop=(dc == DC - 1))
            return ins
        vmm = P.op("tensor", mv,
                   waits=[(S["s_wma" if b == 0 else "s_wmb"], vsl),
                          (dve, max(v_sc, prev_evac))], inc=pe)
        wm_used[b] = vmm
        prev_evac = P.op("vector", lambda e, sw0=sw0, sw1=sw1, b=b: e.tensor_copy(
            rowbuf[0:1, sw0:sw1], ps_mv[0:1, (b % 2) * 512:(b % 2) * 512 + (sw1 - sw0)]),
            waits=[(pe, vmm)], inc=dve)

    v_modl = P.op("gpsimd", dma(modl_d[:, :], rowbuf[0:1, 0:MODSH]),
                  waits=[(dve, prev_evac)], inc=S["s_modg"], inc_k=16)
    v_cc = P.op("gpsimd", lambda e: e.collective_compute(
        "AllGather", ALU.bypass, replica_groups=[list(range(cfg["n_cores"]))],
        ins=[modl_d.ap()], outs=[modg_d.ap()]),
        waits=[(S["s_modg"], v_modl)], inc=S["s_cc"])

    v_tr_prev = 0
    v_ev_prev = v_sc
    for idx, dst, addc in [(0, s1c, 0.0), (1, g1c, 1.0), (3, s2c, 0.0), (4, g2c, 1.0)]:
        wv_ = [(S["s_cc"], v_cc)]
        if v_tr_prev:
            wv_.append((pe, v_tr_prev))
        vld = P.op("sync", dma(
            mslice[:], mod_flat[idx * D:(idx + 1) * D].rearrange("(a p) -> a p", p=128)),
            waits=wv_, inc=S["s_mod"], inc_k=16)
        wv_ = [(S["s_mod"], vld)]
        if v_ev_prev:
            wv_.append((dve, v_ev_prev))
        v_tr_prev = P.op("tensor", lambda e: e.transpose(
            ps_tf[:], mslice[0:DC, :], identf[0:DC, 0:DC]), waits=wv_, inc=pe)
        if addc != 0.0:
            fn = lambda e, dst=dst, a=addc: e.tensor_scalar(dst[:], ps_tf[:], a, None, ALU.add)
        else:
            fn = lambda e, dst=dst: e.tensor_copy(dst[:], ps_tf[:])
        v_ev_prev = P.op("vector", fn, waits=[(pe, v_tr_prev)], inc=dve)
    v_modcols = v_ev_prev

    v_bias = P.op("sync", dma(rowbuf2[0:1, 0:D], bias_in), inc=S["s_row"], inc_k=16)

    def clamp_row(idx):
        vld = P.op("sync", dma(rowbuf[0:1, 0:D], mod_flat[idx * D:(idx + 1) * D]),
                   waits=[(S["s_cc"], v_cc), (dve, dve.n), (S["s_row"], S["s_row"].n)],
                   inc=S["s_row"], inc_k=16)
        P.op("vector", lambda e: e.tensor_scalar(
            scrr_a[0:1, 0:D], rowbuf[0:1, 0:D], -1.0, None, ALU.mult),
            waits=[(S["s_row"], vld)], inc=dve)
        P.op("vector", lambda e: e.tensor_tensor(
            scrr_a[0:1, 0:D], scrr_a[0:1, 0:D], rowbuf[0:1, 0:D], ALU.max), inc=dve)
        P.op("vector", lambda e: e.tensor_scalar(
            scrr_a[0:1, 0:D], scrr_a[0:1, 0:D], 0.25, None, ALU.max), inc=dve)
        P.op("vector", lambda e: e.tensor_scalar(
            scrr_b[0:1, 0:D], rowbuf[0:1, 0:D], 0.0, None, ALU.is_ge), inc=dve)
        P.op("vector", lambda e: e.tensor_scalar(
            scrr_b[0:1, 0:D], scrr_b[0:1, 0:D], 2.0, -1.0, ALU.mult, ALU.add), inc=dve)
        return P.op("vector", lambda e: e.tensor_tensor(
            rowbuf[0:1, 0:D], scrr_a[0:1, 0:D], scrr_b[0:1, 0:D], ALU.mult), inc=dve)

    vga = clamp_row(2)
    P.op("sync", dma(rows_d[0:1, :], rowbuf[0:1, 0:D]),
         waits=[(dve, vga)], inc=S["s_row"], inc_k=16)
    vgb = clamp_row(5)
    v_gb_st = P.op("sync", dma(rows_d[1:2, :], rowbuf[0:1, 0:D]),
                   waits=[(dve, vgb)], inc=S["s_row"], inc_k=16)
    v_gbb = P.op("vector", lambda e: e.tensor_tensor(
        rowbuf[0:1, 0:D], rowbuf[0:1, 0:D], rowbuf2[0:1, 0:D], ALU.mult),
        waits=[(S["s_row"], v_bias), (S["s_row"], v_gb_st)], inc=dve)
    P.op("sync", dma(rows_d[2:3, :], rowbuf[0:1, 0:D]),
         waits=[(dve, v_gbb)], inc=S["s_row"], inc_k=16)
    v_rows_done = S["s_row"].n

    P.flush()
    f0.close()
    if int(os.environ.get("DK_PHASES", "5")) <= 1:
        es.close()
        return nc

    f12 = ExitStack()
    h_fm = sbuf(f12, "h_fm", [128, DC * TCTX], BF16)
    costab = sbuf(f12, "costab_s", [128, TCTX], F32)
    sintab = sbuf(f12, "sintab_s", [128, TCTX], F32)
    halo_b = sbuf(f12, "halo_b", [128, NJT], F32)
    tri_lo = sbuf(f12, "tri_lo_s", [128, 128], BF16)
    tri_hi = sbuf(f12, "tri_hi_s", [128, 128], BF16)
    ones_c = sbuf(f12, "ones_c", [128, 128], BF16)
    RSW = min(1024, TCTX)
    scr_a = sbuf(f12, "scr_a", [128, RSW], F32)
    scr_b = sbuf(f12, "scr_b", [128, RSW], F32)
    tbl_w = [(pe, pe.n), (dve, dve.n), (S["s_row"], v_rows_done),
             (S["s_mod"], S["s_mod"].n), (S["s_modg"], S["s_modg"].n)]
    for dst, srcap in [
            (costab[:], cos_in), (sintab[:], sin_in), (halo_b[:], hb_in),
            (tri_lo[:], trilo_in), (tri_hi[:], trihi_in), (ones_c[:], ones_in)]:
        P.op("sync", dma(dst, srcap), waits=tbl_w, inc=S["s_const"], inc_k=16)
        tbl_w = []
    v_const2 = S["s_const"].n
    f1 = ExitStack()
    ps_t = [f1.enter_context(nc.psum_tensor(f"ps_t{i}", [128, 128], BF16)) for i in range(2)]
    xsl = [sbuf(f1, f"xsl{i}", [128, D], F32) for i in range(2)]
    xnsl = [sbuf(f1, f"xn{i}", [128, D], BF16) for i in range(2)]
    sqscr = sbuf(f1, "sqscr", [128, D], F32)

    def emit_ln(n_tiles, gcol, scol, out_fm, out_stride, x_ap_fn, x_wait_fn,
                x_free_sink, ps_tr, xn_bufs, sq_buf, tile_ready):
        tr_done = [0, 0]
        ev_done = [0, 0]
        xn_free = [0, 0]
        for tt in range(n_tiles):
            b = tt % 2
            xa = x_ap_fn(tt)
            xw = x_wait_fn(tt)
            P.op("vector", lambda e, xa=xa: e.reduce_sum(
                stat[:, 0:1], xa, axis=AX, negate=True), waits=xw, inc=dve)
            P.op("vector", lambda e: e.tensor_scalar(
                stat[:, 1:2], stat[:, 0:1], 1.0 / D, None, ALU.mult), inc=dve)
            va = P.op("scalar", lambda e, xa=xa: e.activation(
                sq_buf[:], xa, AF.Square, accum_out=stat[:, 2:3]),
                waits=xw + [(dve, dve.n)], inc=act)
            P.op("vector", lambda e: e.tensor_tensor(
                stat[:, 3:4], stat[:, 1:2], stat[:, 1:2], ALU.mult),
                waits=[(act, va)], inc=dve)
            P.op("vector", lambda e: e.tensor_scalar(
                stat[:, 4:5], stat[:, 2:3], 1.0 / D, eps, ALU.mult, ALU.add), inc=dve)
            v2 = P.op("vector", lambda e: e.tensor_tensor(
                stat[:, 4:5], stat[:, 4:5], stat[:, 3:4], ALU.subtract), inc=dve)
            P.op("scalar", lambda e: e.activation(
                stat[:, 5:6], stat[:, 4:5], AF.Ln), waits=[(dve, v2)], inc=act)
            v4 = P.op("scalar", lambda e: e.activation(
                stat[:, 6:7], stat[:, 5:6], AF.Exp, scale=-0.5), inc=act)
            wv_ = [(act, v4)]
            if xn_free[b]:
                wv_.append((pe, xn_free[b]))
            v5 = P.op("vector", lambda e, xa=xa, b=b: e.tensor_scalar(
                xn_bufs[b][:], xa, stat[:, 1:2], stat[:, 6:7], ALU.add, ALU.mult),
                waits=wv_, inc=dve)
            x_free_sink(tt, v5)
            for dc in range(DC):
                hb_ = dc % 2
                wv_ = [(dve, max(v5, ev_done[hb_]))]
                vt = P.op("tensor", lambda e, b=b, dc=dc, hb_=hb_: e.transpose(
                    ps_tr[hb_][:],
                    xn_bufs[b][:, dc * 128:(dc + 1) * 128], ident[:]),
                    waits=wv_, inc=pe)
                tr_done[hb_] = vt
                if dc == DC - 1:
                    xn_free[b] = vt
                vv = P.op("vector", lambda e, tt=tt, dc=dc, hb_=hb_: e.tensor_scalar(
                    out_fm[:, dc * out_stride + tt * 128: dc * out_stride + tt * 128 + 128],
                    ps_tr[hb_][:],
                    gcol[:, dc:dc + 1], scol[:, dc:dc + 1], ALU.mult, ALU.add),
                    waits=[(pe, vt)], inc=dve)
                ev_done[hb_] = vv
            tile_ready[tt] = ev_done[(DC - 1) % 2]

    barrier(["sync", "scalar"], [(pe, max(wm_used))])
    barrier(["vector"],
            [(pe, max(wm_used)), (S["s_row"], v_rows_done),
             (S["s_mod"], S["s_mod"].n), (S["s_modg"], S["s_modg"].n)])

    h_tile_ready = [0] * NJT
    x_free = [0, 0]

    def x1_src(tt):
        b = tt % 2
        wv_ = [(dve, x_free[b])] if x_free[b] else []
        S_ = S["s_xa" if b == 0 else "s_xb"]
        P.op("sync", dma(xsl[b][:], x_in[tt * 128:(tt + 1) * 128, :]),
             waits=wv_, inc=S_, inc_k=16)
        return xsl[b][:]

    x1_waits = {}

    def x1_ap(tt):
        ap = x1_src(tt)
        x1_waits[tt] = [(S["s_xa" if tt % 2 == 0 else "s_xb"],
                         S["s_xa" if tt % 2 == 0 else "s_xb"].n)]
        return ap

    emit_ln(NJT, g1c, s1c, h_fm, TCTX,
            x1_ap, lambda tt: x1_waits[tt],
            lambda tt, v: x_free.__setitem__(tt % 2, v),
            ps_t, xnsl, sqscr, h_tile_ready)
    P.flush()
    f1.close()
    if int(os.environ.get("DK_PHASES", "5")) <= 2:
        f12.close()
        es.close()
        return nc

    f2 = ExitStack()
    KQO = 0
    SO = 1024
    AVO = 2048
    DENO = AVO + ((TOWN + 511) // 512) * 512
    assert DENO + TOWN <= 4096
    arena = f2.enter_context(nc.psum_tensor("arena", [128, 4096], F32))

    krot = [sbuf(f2, f"krot{i}", [128, TCTX], BF16) for i in range(2)]
    qrot = [sbuf(f2, f"qrot{i}", [128, TOWN], BF16) for i in range(2)]
    pt_sl = [sbuf(f2, f"pt{i}", [128, PTW], BF16) for i in range(4)]
    v_sb = sbuf(f2, "v_sb", [128, NJT * VEW], BF16)
    wq_sl = [sbuf(f2, f"wqs{i}", [128, DC * 128], BF16) for i in range(2)]
    wk_sl = [sbuf(f2, f"wks{i}", [128, DC * 128], BF16) for i in range(2)]
    wv_sl = sbuf(f2, "wv_sl", [128, DC * VEW], BF16)
    wqk_f = sbuf(f2, "wqk_f", [128, DC * 128], F32)
    wvh_f = sbuf(f2, "wvh_f", [128, DC * 256], F32)
    rec_bc = sbuf(f2, "rec_bc", [128, TOWN], F32)
    ao_sl = [sbuf(f2, f"ao{i}", [128, TOWN], BF16) for i in range(2)]
    facc = sbuf(f2, "facc", [128, DC * EGW], F32)

    barrier(["gpsimd"], [(dve, dve.n), (act, act.n)])

    v_fprev = 0
    for eg in range(D // EGW):
        for r in range(R):
            wv_ = [(S["s_fga"], S["s_fga"].n)]
            if r == 0:
                wv_.append((S["s_fso"], S["s_fso"].n))
            vld = P.op("gpsimd", lambda e, r=r, eg=eg: e.dma_start(
                out=facc[:].rearrange("p (c n) -> p c n", n=EGW),
                in_=fac_in[r, :, eg * EGW:(eg + 1) * EGW].rearrange("(c p) n -> p c n", p=128),
                accum_op=(ALU.bypass if r == 0 else ALU.add)),
                waits=wv_, inc=S["s_fga"], inc_k=16, dma=True)
        vst = P.op("gpsimd", dma(
            fsum_d[:, eg * EGW:(eg + 1) * EGW].rearrange("(c p) n -> p c n", p=128),
            facc[:].rearrange("p (c n) -> p c n", n=EGW)),
            waits=[(S["s_fga"], S["s_fga"].n), (S["s_fso"], S["s_fso"].n)],
            inc=S["s_fso"], inc_k=16, dma=True)
        v_fprev = 0
    v_fsum_all = S["s_fso"].n

    wq_used = [0, 0]
    wk_used = [0, 0]
    wv_used = 0
    qk_stage_free = [0]
    qk_cast_done = {}

    def prefetch_qk(h):
        b = h % 2
        wv_ = [(act, qk_stage_free[0])] if qk_stage_free[0] else []
        vq_d = P.op("sync", dma(
            wqk_f[:].rearrange("p (c n) -> p c n", n=128),
            wq_in[:, h * 128:(h + 1) * 128].rearrange("(c p) n -> p c n", p=128)),
            waits=wv_, inc=S["s_wqa"], inc_k=16)
        cw = [(S["s_wqa"], vq_d)]
        if wq_used[b]:
            cw.append((pe, wq_used[b]))
        vq_c = P.op("scalar", lambda e, b=b: e.copy(wq_sl[b][:], wqk_f[:]),
                    waits=cw, inc=act)
        vk_d = P.op("sync", dma(
            wqk_f[:].rearrange("p (c n) -> p c n", n=128),
            wk_in[:, h * 128:(h + 1) * 128].rearrange("(c p) n -> p c n", p=128)),
            waits=[(act, vq_c)], inc=S["s_wqa"], inc_k=16)
        cw = [(S["s_wqa"], vk_d)]
        if wk_used[b]:
            cw.append((pe, wk_used[b]))
        vk_c = P.op("scalar", lambda e, b=b: e.copy(wk_sl[b][:], wqk_f[:]),
                    waits=cw, inc=act)
        qk_stage_free[0] = vk_c
        qk_cast_done[h] = (vq_c, vk_c)

    prefetch_qk(0)
    prefetch_qk(1)
    ao_stored = [0, 0]
    v_sb_ready = 0
    exp_prev_last = 0
    denav_prev_pe = 0
    denav_prev_dve = 0
    pt_consumed = [0, 0, 0, 0]
    rec_read_prev = 0
    den_dma_prev = 0

    sconsumed = [0, 0, 0]

    wv_cast_done = 0
    wv_prefetched = {}

    def prefetch_wv(g):
        nonlocal wv_cast_done, wv_used
        vwv = 0
        for half in range(VEW // 256):
            wv_ = [(act, wv_cast_done)] if wv_cast_done else []
            c0 = g * VEW + half * 256
            vd = P.op("sync", dma(
                wvh_f[:].rearrange("p (c n) -> p c n", n=256),
                wv_in[:, c0:c0 + 256].rearrange("(c p) n -> p c n", p=128)),
                waits=wv_, inc=S["s_wv"], inc_k=16)
            cw = [(S["s_wv"], vd)]
            if wv_used:
                cw.append((pe, wv_used))
            wv_cast_done = P.op("scalar", lambda e, half=half: e.copy(
                wv_sl[:].rearrange("p (c n) -> p c n", n=VEW)[:, :, half * 256:(half + 1) * 256],
                wvh_f[:].rearrange("p (c n) -> p c n", n=256)),
                waits=cw, inc=act)
        wv_prefetched[g] = wv_cast_done

    prefetch_wv(0)

    kq_bank_free = [0, 0]
    rope_done_k = {}
    rope_done_q = {}
    rope_all = {}
    smm_last_of = {}

    def emit_kqrope(hh):
        vwq, vwk = qk_cast_done[hh]
        rope_done_k[hh] = {}
        rope_done_q[hh] = {}
        guard = [(pe, smm_last_of[hh - 2])] if (hh - 2) in smm_last_of else []
        cidx = 0
        v = 0
        for kind, src_off, width, castv, out, book, cs_off in (
                ("k", 0, TCTX, vwk, krot[hh % 2], rope_done_k[hh], 0),
                ("q", W, TOWN, vwq, qrot[hh % 2], rope_done_q[hh], W)):
            wsl = wk_sl[hh % 2] if kind == "k" else wq_sl[hh % 2]
            for (n0, n1) in splits(0, width, 512):
                bank = cidx % 2
                cidx += 1
                wid = n1 - n0

                def pmm(e, n0=n0, n1=n1, wsl=wsl, src_off=src_off, bank=bank):
                    ins = None
                    for dc in range(DC):
                        ins = e.matmul(
                            arena[:, KQO + bank * 512:KQO + bank * 512 + (n1 - n0)],
                            wsl[:, dc * 128:(dc + 1) * 128],
                            h_fm[:, dc * TCTX + src_off + n0: dc * TCTX + src_off + n1],
                            start=(dc == 0), stop=(dc == DC - 1))
                    return ins
                wts = [(act, castv), (dve, max(h_tile_ready))]
                if kq_bank_free[bank]:
                    wts.append((dve, kq_bank_free[bank]))
                vmmv = P.op("tensor", pmm, waits=wts, inc=pe,
                            tag=f"kq h{hh} {kind} {n0}")
                if kind == "k":
                    wk_used[hh % 2] = vmmv
                else:
                    wq_used[hh % 2] = vmmv
                pv = arena[:, KQO + bank * 512:KQO + bank * 512 + wid]
                wts2 = [(pe, vmmv), (S["s_const"], v_const2)] + guard
                P.op("vector", lambda e, pv=pv, n0=n0, wid=wid, cs_off=cs_off: e.tensor_tensor(
                    scr_a[:, 0:wid], pv,
                    costab[:, cs_off + n0:cs_off + n0 + wid], ALU.mult), waits=wts2)
                P.op("vector", lambda e, pv=pv, n0=n0, wid=wid, cs_off=cs_off: e.tensor_tensor(
                    scr_b[0:64, 0:wid], pv[64:128, :],
                    sintab[0:64, cs_off + n0:cs_off + n0 + wid], ALU.mult))
                P.op("vector", lambda e, pv=pv, n0=n0, wid=wid, cs_off=cs_off: e.tensor_tensor(
                    scr_b[64:128, 0:wid], pv[0:64, :],
                    sintab[64:128, cs_off + n0:cs_off + n0 + wid], ALU.mult))
                v = P.op("vector", lambda e, out=out, n0=n0, wid=wid: e.tensor_tensor(
                    out[:, n0:n0 + wid], scr_a[:, 0:wid], scr_b[:, 0:wid], ALU.add),
                    tag=f"rope h{hh} {kind} {n0}")
                book[n0] = v
                kq_bank_free[bank] = v
        rope_all[hh] = v

    emit_kqrope(0)
    if H > 2:
        prefetch_qk(2)

    for h in range(H):
        g, hg = divmod(h, HG)
        if hg == 0:
            vwv = wv_prefetched[g]
            v_ev = 0
            for jt in range(NJT):
                sb_ = jt % 2

                def vmm(e, jt=jt, sb_=sb_):
                    ins = None
                    for dc in range(DC):
                        ins = e.matmul(arena[:, SO + sb_ * 512:SO + sb_ * 512 + VEW],
                                       h_fm[:, dc * TCTX + jt * 128: dc * TCTX + jt * 128 + 128],
                                       wv_sl[:, dc * VEW:(dc + 1) * VEW],
                                       start=(dc == 0), stop=(dc == DC - 1))
                    return ins
                wv2 = [(act, vwv), (dve, h_tile_ready[jt]),
                       (act, max(sconsumed[sb_], exp_prev_last if jt < 2 else 0))]
                if jt >= 2:
                    wv2.append((dve, v_ev))
                vp = P.op("tensor", vmm, waits=wv2, inc=pe, tag=f"vmm g{g} jt{jt}")
                wv_used = vp
                v_ev = P.op("vector", lambda e, jt=jt, sb_=sb_: e.tensor_copy(
                    v_sb[:, jt * VEW:(jt + 1) * VEW], arena[:, SO + sb_ * 512:SO + sb_ * 512 + VEW]),
                    waits=[(pe, vp)], inc=dve)
            v_sb_ready = v_ev

        if h + 1 < H:
            emit_kqrope(h + 1)

        si = 0
        den_started = [False] * max(1, TOWN // 512)
        av_started = [False] * max(1, TOWN // 512)
        last_pe = 0
        pending = None

        _order = [W // 128] + [j for j in range(NJT) if j != W // 128]
        _chunk_seq = []
        for _jt in _order:
            _j0 = _jt * 128
            _A0 = max(0, _j0 - W)
            _A1 = min(TOWN, _j0 + 128)
            for (_a0, _a1) in splits(_A0, _A1, 512):
                _chunk_seq.append((_jt, _a0, _a1))
        _last_of_bank = {}
        for _i, (_jt, _a0, _a1) in enumerate(_chunk_seq):
            _last_of_bank[_a0 // 512] = _i
        _seq_idx = [0]

        def emit_denav(jt, A0, chunks, v_exp, v_tri, pos=None, h=h, hg=hg):
            nonlocal last_pe
            ptl = pt_sl[pos % 4]
            for (a0, a1) in chunks:
                di = a0 // 512
                idx = _seq_idx[0]
                _seq_idx[0] += 1
                assert _chunk_seq[idx] == (jt, a0, a1)
                isl = (_last_of_bank[di] == idx)

                def dmm(e, a0=a0, a1=a1, ptl=ptl, A0=A0,
                        st=(not den_started[di]), sp=isl):
                    return e.matmul(arena[:, DENO + a0:DENO + a1],
                                    ones_c[:], ptl[:, a0 - A0:a1 - A0],
                                    start=st, stop=sp)
                wts = [(act, v_exp), (dve, v_tri)]
                if h > 0 and not any(den_started) and not any(av_started):
                    wts.append((dve, denav_prev_dve))
                P.op("tensor", dmm, waits=wts, inc=pe, tag=f"den h{h} jt{jt} q{a0}")
                den_started[di] = True

                def amm(e, jt=jt, A0=A0, a0=a0, a1=a1, hg=hg, ptl=ptl,
                        st=(not av_started[di]), sp=isl):
                    return e.matmul(arena[:, AVO + a0:AVO + a1],
                                    v_sb[:, jt * VEW + hg * 128: jt * VEW + hg * 128 + 128],
                                    ptl[:, a0 - A0:a1 - A0], start=st, stop=sp)
                wts2 = [(dve, v_sb_ready), (act, v_exp), (dve, v_tri)]
                if h > 0 and not any(av_started):
                    wts2.append((dve, denav_prev_dve))
                last_pe = P.op("tensor", amm, waits=wts2, inc=pe,
                               tag=f"av h{h} jt{jt} q{a0}")
                av_started[di] = True

        for pos, jt in enumerate(_order):
            j0 = jt * 128
            A0 = max(0, j0 - W)
            A1 = min(TOWN, j0 + 128)
            if pos == 0:
                assert (A0, A1) == (0, TOWN)
            chunks = splits(A0, A1, 512)
            ptl = pt_sl[pos % 4]
            s_mm = []
            for (a0, a1) in chunks:
                slx = si % 2
                si += 1

                def smm(e, jt=jt, a0=a0, a1=a1, slx=slx, h=h):
                    return e.matmul(arena[:, SO + slx * 512:SO + slx * 512 + (a1 - a0)],
                                    krot[h % 2][:, jt * 128:jt * 128 + 128],
                                    qrot[h % 2][:, a0:a1], start=True, stop=True)
                wts = [(dve, rope_done_k[h][(jt * 128 // 512) * 512]),
                       (dve, rope_done_q[h][(a0 // 512) * 512])]
                if sconsumed[slx]:
                    wts.append((act, sconsumed[slx]))
                if hg == 0:
                    wts.append((dve, v_sb_ready))
                vs = P.op("tensor", smm, waits=wts, inc=pe,
                          tag=f"smm h{h} jt{jt} a{a0}")
                s_mm.append((a0, a1, vs, slx))
            smm_last = s_mm[-1][2]
            vexp = 0
            for (a0, a1, vs, slx) in s_mm:
                def expf(e, jt=jt, a0=a0, a1=a1, slx=slx, A0=A0, ptl=ptl):
                    return e.activation(ptl[:, a0 - A0:a1 - A0],
                                        arena[:, SO + slx * 512:SO + slx * 512 + (a1 - a0)],
                                        AF.Exp, bias=halo_b[:, jt:jt + 1])
                wts = [(pe, vs)]
                if pt_consumed[pos % 4]:
                    wts.append((pe, pt_consumed[pos % 4]))
                vexp = P.op("scalar", expf, waits=wts, inc=act,
                            tag=f"exp h{h} jt{jt} a{a0}")
                sconsumed[slx] = vexp
            tris = []
            if j0 < TOWN:
                tris.append((j0 - A0, tri_lo))
            if j0 >= W:
                tris.append((0, tri_hi))
            vt_ = 0
            for toff, tmask in tris:
                vt_ = P.op("vector",
                           lambda e, toff=toff, tmask=tmask, ptl=ptl: e.tensor_tensor(
                               ptl[:, toff:toff + 128], ptl[:, toff:toff + 128],
                               tmask[:], ALU.mult),
                           waits=[(act, vexp)], inc=dve, tag=f"tri h{h} jt{jt}")
            if pending is not None:
                emit_denav(*pending[1:], pos=pending[0])
                pt_consumed[pending[0] % 4] = last_pe
            pending = (pos, jt, A0, chunks, vexp, vt_)
        smm_last_of[h] = smm_last
        if h + 3 < H:
            prefetch_qk(h + 3)
        if hg == 2 and g + 1 < H // HG:
            prefetch_wv(g + 1)
        emit_denav(*pending[1:], pos=pending[0])
        pt_consumed[pending[0] % 4] = last_pe
        pending = None
        exp_prev_last = sconsumed[(si - 1) % 2]

        wts = [(pe, last_pe)]
        if rec_read_prev:
            wts.append((dve, rec_read_prev))
        vdc = P.op("vector", lambda e: e.tensor_copy(
            rec_bc[:], arena[:, DENO:DENO + TOWN]), waits=wts, inc=dve)
        vdb = P.op("vector", lambda e: e.reciprocal(rec_bc[:], rec_bc[:]),
                   waits=[(dve, vdc)], inc=dve)

        b = h % 2
        wts = [(pe, last_pe), (dve, vdb)]
        if ao_stored[b]:
            wts.append((S["s_aoa" if b == 0 else "s_aob"], ao_stored[b]))
        vev = P.op("vector", lambda e, b=b: e.scalar_tensor_tensor(
            ao_sl[b][:], arena[:, AVO:AVO + TOWN], 1.0, rec_bc[:], ALU.mult, ALU.mult),
            waits=wts, inc=dve)
        rec_read_prev = vev
        denav_prev_dve = vev
        ao_stored[b] = P.op("sync", dma(ao_d[h * 128:(h + 1) * 128, :], ao_sl[b][:]),
                            waits=[(dve, vev)],
                            inc=S["s_aoa" if b == 0 else "s_aob"], inc_k=16)

    v_ao_all = (S["s_aoa"].n, S["s_aob"].n)
    P.flush()
    f2.close()
    f12.close()
    if int(os.environ.get("DK_PHASES", "5")) <= 3:
        es.close()
        return nc

    f34 = ExitStack()
    x1_sb = sbuf(f34, "x1_sb", [128, NQT * D], F32)
    h2_fm = sbuf(f34, "h2_fm", [128, DC * TOWN], BF16)
    gb_b = sbuf(f34, "gb_b", [128, D], BF16)
    gbb_b = sbuf(f34, "gbb_b", [128, D], BF16)
    f3 = ExitStack()
    ps_t3 = [f3.enter_context(nc.psum_tensor(f"ps_t3{i}", [128, 128], BF16)) for i in range(2)]
    ps_wo = f3.enter_context(nc.psum_tensor("ps_wo", [128, 1024], F32))
    ao_sb = sbuf(f3, "ao_sb", [128, DC * TOWN], BF16)
    wo_sl = [sbuf(f3, f"wo_sl{i}", [128, DC * 512], BF16) for i in range(2)]
    wo_f = sbuf(f3, "wo_f", [128, DC * 256], F32)
    ga_b = sbuf(f3, "ga_b", [128, D], BF16)
    xsl3 = [sbuf(f3, f"x3_{i}", [128, 512], F32) for i in range(2)]
    xnsl3 = [sbuf(f3, f"xn3_{i}", [128, D], BF16) for i in range(2)]
    sqscr3 = sbuf(f3, "sqscr3", [128, D], BF16)
    t1scr = sbuf(f3, "t1scr", [128, 512], F32)

    barrier(["sync", "gpsimd", "tensor", "vector", "scalar"],
            [(pe, pe.n), (dve, dve.n), (act, act.n), (gp, gp.n),
             (S["s_fso"], v_fsum_all), (S["s_aoa"], v_ao_all[0]),
             (S["s_aob"], v_ao_all[1]),
             (S["s_den"], S["s_den"].n), (S["s_denb"], S["s_denb"].n)])

    P.op("gpsimd", dma(ga_b[:], rows_d[0:1, :].broadcast_to([128, D])),
         waits=[(S["s_row"], v_rows_done)], inc=S["s_rowg"], inc_k=16, dma=True)
    P.op("gpsimd", dma(gb_b[:], rows_d[1:2, :].broadcast_to([128, D])),
         inc=S["s_rowg"], inc_k=16, dma=True)
    P.op("gpsimd", dma(gbb_b[:], rows_d[2:3, :].broadcast_to([128, D])),
         inc=S["s_rowg"], inc_k=16, dma=True)
    vgb_b = vgbb_b = None
    v_aor = P.op("sync", dma(ao_sb[:].rearrange("p (c n) -> p c n", n=TOWN),
                             ao_d.ap().rearrange("(c p) n -> p c n", p=128)),
                 waits=[(S["s_aoa"], v_ao_all[0]), (S["s_aob"], v_ao_all[1])],
                 inc=S["s_aor"], inc_k=16)

    NDG = D // 512
    wo_used = 0
    x1_ready = [0] * NQT
    h2_tile_ready = [0] * NQT
    _ln2_state = dict(tr=[0, 0], ev=[0, 0], xf=[0, 0])

    def emit_ln2_tile(tt):
        b = tt % 2
        xa = x1_sb[:, tt * D:(tt + 1) * D]
        xw = [(dve, x1_ready[tt])]
        P.op("vector", lambda e, xa=xa: e.reduce_sum(
            stat[:, 0:1], xa, axis=AX, negate=True), waits=xw, inc=dve)
        P.op("vector", lambda e: e.tensor_scalar(
            stat[:, 1:2], stat[:, 0:1], 1.0 / D, None, ALU.mult), inc=dve)
        va = P.op("scalar", lambda e, xa=xa: e.activation(
            sqscr3[:], xa, AF.Square, accum_out=stat[:, 2:3]),
            waits=xw + [(dve, dve.n)], inc=act)
        P.op("vector", lambda e: e.tensor_tensor(
            stat[:, 3:4], stat[:, 1:2], stat[:, 1:2], ALU.mult),
            waits=[(act, va)], inc=dve)
        P.op("vector", lambda e: e.tensor_scalar(
            stat[:, 4:5], stat[:, 2:3], 1.0 / D, eps, ALU.mult, ALU.add), inc=dve)
        v2 = P.op("vector", lambda e: e.tensor_tensor(
            stat[:, 4:5], stat[:, 4:5], stat[:, 3:4], ALU.subtract), inc=dve)
        P.op("scalar", lambda e: e.activation(
            stat[:, 5:6], stat[:, 4:5], AF.Ln), waits=[(dve, v2)], inc=act)
        v4 = P.op("scalar", lambda e: e.activation(
            stat[:, 6:7], stat[:, 5:6], AF.Exp, scale=-0.5), inc=act)
        wv_ = [(act, v4)]
        if _ln2_state["xf"][b]:
            wv_.append((pe, _ln2_state["xf"][b]))
        v5 = P.op("vector", lambda e, xa=xa, b=b: e.tensor_scalar(
            xnsl3[b][:], xa, stat[:, 1:2], stat[:, 6:7], ALU.add, ALU.mult),
            waits=wv_, inc=dve)
        for dc in range(DC):
            hb_ = dc % 2
            wv_ = [(dve, max(v5, _ln2_state["ev"][hb_]))]
            vt = P.op("tensor", lambda e, b=b, dc=dc, hb_=hb_: e.transpose(
                ps_t3[hb_][:],
                xnsl3[b][:, dc * 128:(dc + 1) * 128], ident[:]),
                waits=wv_, inc=pe)
            _ln2_state["tr"][hb_] = vt
            if dc == DC - 1:
                _ln2_state["xf"][b] = vt
            vv = P.op("vector", lambda e, tt=tt, dc=dc, hb_=hb_: e.tensor_scalar(
                h2_fm[:, dc * TOWN + tt * 128: dc * TOWN + tt * 128 + 128],
                ps_t3[hb_][:],
                g2c[:, dc:dc + 1], s2c[:, dc:dc + 1], ALU.mult, ALU.add),
                waits=[(pe, vt)], inc=dve)
            _ln2_state["ev"][hb_] = vv
        h2_tile_ready[tt] = _ln2_state["ev"][(DC - 1) % 2]
    x3_done = [0, 0]
    x3_loaded = [0, 0]
    wo_evac = [0, 0]
    wo_cast_done = 0
    wo_cast_of = {}
    wo_mm_of = {}

    def prefetch_wo(dg):
        nonlocal wo_cast_done
        for half in range(2):
            wv_ = [(act, wo_cast_done)] if wo_cast_done else []
            c0 = dg * 512 + half * 256
            vd = P.op("sync", dma(
                wo_f[:].rearrange("p (c n) -> p c n", n=256),
                wo_in[:, c0:c0 + 256].rearrange("(c p) n -> p c n", p=128)),
                waits=wv_, inc=S["s_wo"], inc_k=16)
            cw = [(S["s_wo"], vd)]
            if dg - 2 in wo_mm_of:
                cw.append((pe, wo_mm_of[dg - 2]))
            wo_cast_done = P.op("scalar", lambda e, half=half, dg=dg: e.copy(
                wo_sl[dg % 2][:].rearrange("p (c n) -> p c n", n=512)[:, :, half * 256:(half + 1) * 256],
                wo_f[:].rearrange("p (c n) -> p c n", n=256)),
                waits=cw, inc=act)
        wo_cast_of[dg] = wo_cast_done

    prefetch_wo(0)
    for dg in range(NDG):
        if dg + 1 < NDG:
            prefetch_wo(dg + 1)
        vwo = wo_cast_of[dg]
        for tt in range(NQT):
            bb = tt % 2
            wv_ = [(dve, x3_done[bb])] if x3_done[bb] else []
            v_x3 = P.op(
                "sync", dma(xsl3[bb][:],
                            x_in[W + tt * 128:W + (tt + 1) * 128, dg * 512:(dg + 1) * 512]),
                waits=wv_, inc=S["s_xoa" if bb == 0 else "s_xob"], inc_k=16)

            def womm(e, tt=tt, dg=dg):
                ins = None
                for ec in range(DC):
                    ins = e.matmul(ps_wo[:, (tt % 2) * 512:(tt % 2) * 512 + 512],
                                   ao_sb[:, ec * TOWN + tt * 128: ec * TOWN + tt * 128 + 128],
                                   wo_sl[dg % 2][:, ec * 512:(ec + 1) * 512],
                                   start=(ec == 0), stop=(ec == DC - 1))
                return ins
            wts = [(act, vwo), (S["s_aor"], v_aor)]
            if wo_evac[tt % 2]:
                wts.append((dve, wo_evac[tt % 2]))
            vwomm = P.op("tensor", womm, waits=wts, inc=pe)
            wo_used = vwomm
            wo_mm_of[dg] = vwomm

            wts = [(pe, vwomm), (S["s_xoa" if bb == 0 else "s_xob"], v_x3),
                   (S["s_rowg"], S["s_rowg"].n)]
            P.op("vector", lambda e, tt=tt, dg=dg: e.scalar_tensor_tensor(
                t1scr[:], ps_wo[:, (tt % 2) * 512:(tt % 2) * 512 + 512],
                1.0, ga_b[:, dg * 512:(dg + 1) * 512], ALU.mult, ALU.mult),
                waits=wts)
            wo_evac[tt % 2] = P.op("vector", lambda e, tt=tt, dg=dg: e.tensor_tensor(
                x1_sb[:, tt * D + dg * 512: tt * D + (dg + 1) * 512],
                t1scr[:], xsl3[tt % 2][:], ALU.add))
            x3_done[bb] = wo_evac[tt % 2]
            if dg == NDG - 1:
                x1_ready[tt] = wo_evac[tt % 2]
                emit_ln2_tile(tt)

    P.flush()
    f3.close()
    if int(os.environ.get("DK_PHASES", "5")) <= 4:
        f34.close()
        es.close()
        return nc

    f4 = ExitStack()
    NEGF = D // 512
    ps_ff = f4.enter_context(nc.psum_tensor("ps_ff", [128, 512 * NEGF], F32))
    f_sb = sbuf(f4, "f_sb", [128, DC * D], BF16)
    out_sl = [sbuf(f4, f"out_sl{i}", [128, D], F32) for i in range(2)]
    t1scr4 = sbuf(f4, "t1scr4", [128, 512], F32)

    barrier(["sync", "tensor", "vector", "scalar"],
            [(pe, pe.n), (dve, dve.n), (act, act.n),
             (S["s_xoa"], S["s_xoa"].n), (S["s_xob"], S["s_xob"].n),
             (S["s_aor"], S["s_aor"].n)])

    for sl4 in range(4):
        c0 = sl4 * (D // 4)
        P.op("sync", dma(
            f_sb[:].rearrange("p (c n) -> p c n", n=D)[:, :, c0:c0 + D // 4],
            fsum_d.ap()[:, c0:c0 + D // 4].rearrange("(c p) n -> p c n", p=128)),
            waits=[(S["s_fso"], v_fsum_all)], inc=S["s_fr"], inc_k=16)
    v_fr = S["s_fr"].n

    x1p_ready = [0] * NQT
    for tt in range(NQT):
        x1p_ready[tt] = P.op("vector", lambda e, tt=tt: e.tensor_tensor(
            x1_sb[:, tt * D:(tt + 1) * D], x1_sb[:, tt * D:(tt + 1) * D],
            gbb_b[:], ALU.add),
            waits=[(dve, h2_tile_ready[tt]), (S["s_rowg"], S["s_rowg"].n)], inc=dve)

    ff_evac = 0
    out_dma_of = [0, 0]
    for tt in range(NQT):
        def ffmm(e, tt=tt):
            ins = None
            for dc in range(DC):
                for eg in range(NEGF):
                    ins = e.matmul(ps_ff[:, eg * 512:(eg + 1) * 512],
                                   h2_fm[:, dc * TOWN + tt * 128: dc * TOWN + tt * 128 + 128],
                                   f_sb[:, dc * D + eg * 512:dc * D + (eg + 1) * 512],
                                   start=(dc == 0), stop=(dc == DC - 1))
            return ins
        wts = [(S["s_fr"], v_fr), (dve, h2_tile_ready[tt])]
        if ff_evac:
            wts.append((dve, ff_evac))
        vf = P.op("tensor", ffmm, waits=wts, inc=pe)

        last_ev = 0
        ob = tt % 2
        for eg in range(NEGF):
            wts = [(pe, vf), (dve, x1p_ready[tt]), (S["s_rowg"], S["s_rowg"].n)]
            if eg == 0 and out_dma_of[ob]:
                wts.append((S["s_ya" if ob == 0 else "s_yb"], out_dma_of[ob]))
            P.op("vector", lambda e, eg=eg: e.scalar_tensor_tensor(
                t1scr4[:], ps_ff[:, eg * 512:(eg + 1) * 512],
                1.0 / math.sqrt(R), gb_b[:, eg * 512:(eg + 1) * 512],
                ALU.mult, ALU.mult),
                waits=wts)
            last_ev = P.op("vector", lambda e, tt=tt, eg=eg, ob=ob: e.tensor_tensor(
                out_sl[ob][:, eg * 512:(eg + 1) * 512], t1scr4[:],
                x1_sb[:, tt * D + eg * 512: tt * D + (eg + 1) * 512], ALU.add))
        ff_evac = last_ev
        out_dma_of[ob] = P.op("sync", dma(y_out[tt * 128:(tt + 1) * 128, :], out_sl[ob][:]),
                              waits=[(dve, last_ev)],
                              inc=S["s_ya" if ob == 0 else "s_yb"], inc_k=16)

    P.op("sync", lambda e: None,
         waits=[(S["s_ya"], S["s_ya"].n), (S["s_yb"], S["s_yb"].n)])
    P.flush()
    f4.close()
    f34.close()
    es.close()
    return nc


CFG = make_cfg()


def shard_inputs(cfg, full):
    D, TOWN, W, TCTX = cfg["D"], cfg["TOWN"], cfg["W"], cfg["TCTX"]
    n = cfg["n_cores"]
    x = np.ascontiguousarray(np.asarray(full["x"], dtype=np.float32)).reshape(-1, D)
    maps = []
    for i in range(n):
        t0 = i * TOWN - W
        if t0 < 0:
            xs = np.concatenate([np.zeros((-t0, D), np.float32), x[0:TCTX + t0]], 0)
        else:
            xs = x[t0:t0 + TCTX]
        tabs = host_tables(cfg, i)
        maps.append(dict(
            x=np.ascontiguousarray(xs),
            wq=full["wq"], wk=full["wk"], wv=full["wv"], wo=full["wo"],
            factors=full["factors"], fbias=full["fbias"].reshape(1, D),
            cvec=full["cvec"].reshape(1, D),
            wmod=np.ascontiguousarray(full["wmod"][:, i * cfg["MODSH"]:(i + 1) * cfg["MODSH"]]),
            costab=tabs["costab"], sintab=tabs["sintab"], halo_bias=tabs["halo_bias"],
            tri_lo=tabs["tri_lo"], tri_hi=tabs["tri_hi"], ident=tabs["ident"],
            identf=tabs["identf"], ones_col=tabs["ones_col"]))
    return maps


def run(cfg, full):
    maps = shard_inputs(cfg, full)
    nc = build_program(cfg)
    res = run_bass_kernel_spmd(nc, maps, core_ids=list(range(cfg["n_cores"])))
    y = np.concatenate([np.asarray(r["y"]) for r in res.results], axis=0)
    return y.reshape(1, cfg["n_cores"] * cfg["TOWN"], cfg["D"]).astype(np.float32)


def kernel(L0_attn_hidden_states, L0_attn_wq, L0_attn_wk, L0_attn_wv,
           L0_attn_wo, L0_ffn_factors, L0_ffn_bias, L0_c, L0_wmod):
    full = dict(x=L0_attn_hidden_states,
                wq=np.asarray(L0_attn_wq, np.float32),
                wk=np.asarray(L0_attn_wk, np.float32),
                wv=np.asarray(L0_attn_wv, np.float32),
                wo=np.asarray(L0_attn_wo, np.float32),
                factors=np.asarray(L0_ffn_factors, np.float32),
                fbias=np.asarray(L0_ffn_bias, np.float32),
                cvec=np.asarray(L0_c, np.float32),
                wmod=np.asarray(L0_wmod, np.float32))
    return run(CFG, full)


# version 27
# speedup vs baseline: 1.0562x; 1.0037x over previous
import math
import os
import numpy as np
from contextlib import ExitStack

import concourse.bass as bass
import concourse.mybir as mybir
from concourse.bass_utils import run_bass_kernel_spmd
import ml_dtypes

BF = ml_dtypes.bfloat16
F32 = mybir.dt.float32
BF16 = mybir.dt.bfloat16
AF = mybir.ActivationFunctionType
ALU = mybir.AluOpType
AX = mybir.AxisListType.X

NEG_BIG = -1.0e9


def make_cfg(D=2048, TOWN=1024, W=1024, R=8, n_cores=8):
    c = dict(D=D, TOWN=TOWN, W=W, R=R, n_cores=n_cores)
    c["HD"] = 128
    c["H"] = D // 128
    c["DC"] = D // 128
    c["TCTX"] = TOWN + W
    c["NJT"] = c["TCTX"] // 128
    c["NQT"] = TOWN // 128
    c["MODW"] = 6 * D
    c["MODSH"] = c["MODW"] // n_cores
    c["theta"] = 10000.0
    c["eps"] = 1e-5
    c["EGW"] = 256
    c["HG"] = min(4, c["H"])
    return c


def splits(off, end, step):
    out = []
    a = off
    while a < end:
        b = min(end, (a // step + 1) * step)
        out.append((a, b))
        a = b
    return out


def host_tables(cfg, core):
    TOWN, W, TCTX = cfg["TOWN"], cfg["W"], cfg["TCTX"]
    HD = cfg["HD"]
    t0 = TOWN * core - W
    pos = np.arange(TCTX, dtype=np.float64) + t0
    inv = 1.0 / (cfg["theta"] ** (np.arange(0, HD, 2, dtype=np.float64) / HD))
    fr = pos[:, None] * inv[None, :]
    s4 = float(HD) ** -0.25
    cos = (np.cos(fr) * s4).astype(np.float32).T
    sin = (np.sin(fr) * s4).astype(np.float32).T
    costab = np.ascontiguousarray(np.concatenate([cos, cos], 0))
    sintab = np.ascontiguousarray(np.concatenate([-sin, sin], 0))
    hb = np.zeros((128, cfg["NJT"]), dtype=np.float32)
    if core == 0:
        hb[:, : W // 128] = NEG_BIG
    p = np.arange(128)
    tri_lo = (p[None, :] < p[:, None]).astype(BF)
    tri_hi = (p[None, :] >= p[:, None]).astype(BF)
    return dict(costab=costab, sintab=sintab, halo_bias=hb, tri_lo=tri_lo,
                tri_hi=tri_hi, ident=np.eye(128, dtype=BF),
                identf=np.eye(128, dtype=np.float32),
                ones_col=np.ones((128, 128), dtype=BF))


class SemT:
    def __init__(self, nc, es, name):
        self.h = es.enter_context(nc.semaphore(name))
        self.n = 0


class Prog:
    ENGS = ("tensor", "vector", "scalar", "gpsimd", "sync")
    TAGS = []
    WSEEN = {}
    AUTO = {"vector": "dve", "scalar": "act", "gpsimd": "gp"}

    def __init__(self, nc, esem):
        self.nc = nc
        self.ops = []
        self.esem = esem
        Prog.WSEEN = {}
        Prog.TAGS = []

    def op(self, eng, fn, waits=(), inc=None, inc_k=1, dma=False, tag=""):
        w = [(s, v) for (s, v) in waits if v > 0]
        if eng in self.AUTO:
            sem = self.esem[self.AUTO[eng]]
            if sem.n > 0:
                w.append((sem, sem.n))
            if inc is None and not dma:
                inc, inc_k = sem, 1
        self.ops.append((eng, fn, w, inc, inc_k, tag))
        if inc is not None:
            inc.n += inc_k
            return inc.n
        return None

    def flush(self):
        byeng = {e: [] for e in self.ENGS}
        for o in self.ops:
            byeng[o[0]].append(o)
        with self.nc.Block() as block:
            for ename in self.ENGS:
                ops = byeng[ename]
                if not ops:
                    continue

                def emit(e, ops=ops, ename=ename):
                    seen = Prog.WSEEN.setdefault(ename, {})
                    for (_, fn, waits, inc, inc_k, tag) in ops:
                        n0 = self.nc.n_instructions()
                        for (s, v) in waits:
                            if seen.get(id(s), -1) >= v:
                                continue
                            seen[id(s)] = v
                            e.wait_ge(s.h, v)
                        ins = fn(e)
                        if inc is not None:
                            assert ins is not None, "op with inc returned None"
                            ins.then_inc(inc.h, inc_k)
                        if tag:
                            n1 = self.nc.n_instructions()
                            Prog.TAGS.append((n0, n1, tag))
                getattr(block, ename)(emit)
        self.ops = []


def build_program(cfg):
    D, TOWN, W, TCTX, R = cfg["D"], cfg["TOWN"], cfg["W"], cfg["TCTX"], cfg["R"]
    H, DC = cfg["H"], cfg["DC"]
    NJT, NQT = cfg["NJT"], cfg["NQT"]
    MODSH = cfg["MODSH"]
    EGW, HG = cfg["EGW"], cfg["HG"]
    PTW = min(TOWN, W + 128)
    VEW = HG * 128
    eps = cfg["eps"]

    nc = bass.Bass("TRN2", target_bir_lowering=False)
    es = ExitStack()

    x_in = nc.dram_tensor("x", [TCTX, D], F32, kind="ExternalInput").ap()
    wq_in = nc.dram_tensor("wq", [D, D], F32, kind="ExternalInput").ap()
    wk_in = nc.dram_tensor("wk", [D, D], F32, kind="ExternalInput").ap()
    wv_in = nc.dram_tensor("wv", [D, D], F32, kind="ExternalInput").ap()
    wo_in = nc.dram_tensor("wo", [D, D], F32, kind="ExternalInput").ap()
    fac_in = nc.dram_tensor("factors", [R, D, D], F32, kind="ExternalInput").ap()
    bias_in = nc.dram_tensor("fbias", [1, D], F32, kind="ExternalInput").ap()
    c_in = nc.dram_tensor("cvec", [1, D], F32, kind="ExternalInput").ap()
    wmod_in = nc.dram_tensor("wmod", [D, MODSH], F32, kind="ExternalInput").ap()
    cos_in = nc.dram_tensor("costab", [128, TCTX], F32, kind="ExternalInput").ap()
    sin_in = nc.dram_tensor("sintab", [128, TCTX], F32, kind="ExternalInput").ap()
    hb_in = nc.dram_tensor("halo_bias", [128, NJT], F32, kind="ExternalInput").ap()
    trilo_in = nc.dram_tensor("tri_lo", [128, 128], BF16, kind="ExternalInput").ap()
    trihi_in = nc.dram_tensor("tri_hi", [128, 128], BF16, kind="ExternalInput").ap()
    ident_in = nc.dram_tensor("ident", [128, 128], BF16, kind="ExternalInput").ap()
    identf_in = nc.dram_tensor("identf", [128, 128], F32, kind="ExternalInput").ap()
    ones_in = nc.dram_tensor("ones_col", [128, 128], BF16, kind="ExternalInput").ap()
    y_out = nc.dram_tensor("y", [TOWN, D], F32, kind="ExternalOutput").ap()

    modl_d = nc.dram_tensor("modl_d", [1, MODSH], F32)
    modg_d = nc.dram_tensor("modg_d", [cfg["n_cores"], MODSH], F32)
    rows_d = nc.dram_tensor("rows_d", [4, D], F32)
    dens_d = nc.dram_tensor("dens_d", [H, TOWN], F32)
    ao_d = nc.dram_tensor("ao_d", [D, TOWN], BF16)
    fsum_d = nc.dram_tensor("fsum_d", [D, D], BF16)
    mod_flat = modg_d.ap().rearrange("a b -> (a b)")

    sem_names = [
        "pe", "dve", "act", "gp",
        "s_const", "s_xa", "s_xb", "s_wma", "s_wmb",
        "s_wqa", "s_wqb", "s_wka", "s_wkb", "s_wv",
        "s_fga", "s_fgb", "s_fso", "s_mod", "s_den", "s_denb", "s_aoa", "s_aob", "s_aor",
        "s_wo", "s_xoa", "s_xob", "s_fr", "s_ya", "s_yb", "s_row", "s_rowg", "s_cc", "s_modg",
    ]
    S = {n: SemT(nc, es, n) for n in sem_names}
    pe, dve, act, gp = S["pe"], S["dve"], S["act"], S["gp"]

    def sbuf(stack, name, shape, dt):
        return stack.enter_context(nc.sbuf_tensor(name, shape, dt))

    ident = sbuf(es, "identb", [128, 128], BF16)
    identf = sbuf(es, "identf_s", [128, 128], F32)
    g1c = sbuf(es, "g1c", [128, DC], F32)
    s1c = sbuf(es, "s1c", [128, DC], F32)
    g2c = sbuf(es, "g2c", [128, DC], F32)
    s2c = sbuf(es, "s2c", [128, DC], F32)
    siluc = sbuf(es, "siluc", [128, DC], F32)
    stat = sbuf(es, "stat", [128, 10], F32)

    P = Prog(nc, {"dve": S["dve"], "act": S["act"], "gp": S["gp"]})

    def dma(dst, src):
        return lambda e: e.dma_start(out=dst, in_=src)

    def barrier(engines, waits):
        for eng in engines:
            P.op(eng, lambda e: None, waits=list(waits), dma=True)

    f0 = ExitStack()
    SW = min(512, MODSH)
    NSL = (MODSH + SW - 1) // SW
    ps_mv = f0.enter_context(nc.psum_tensor("ps_mv", [1, 1024], F32))
    ps_tf = f0.enter_context(nc.psum_tensor("ps_tf", [128, DC], F32))
    wm_sl = [sbuf(f0, f"wm{i}", [128, DC * SW], F32) for i in range(2)]
    rowbuf = sbuf(f0, "rowbuf", [1, max(D, MODSH)], F32)
    rowbuf2 = sbuf(f0, "rowbuf2", [1, D], F32)
    ctile = sbuf(f0, "ctile", [DC, 128], F32)
    mslice = sbuf(f0, "mslice", [DC, 128], F32)
    scrr_a = sbuf(f0, "scrr_a", [1, D], F32)
    scrr_b = sbuf(f0, "scrr_b", [1, D], F32)

    for dst, srcap in [
            (ident[:], ident_in), (identf[:], identf_in),
            (ctile[:], c_in.rearrange("u (a p) -> (u a) p", p=128))]:
        P.op("sync", dma(dst, srcap), inc=S["s_const"], inc_k=16)
    v_const = S["s_const"].n

    wm_used = [0, 0]

    v_silu = P.op("scalar", lambda e: e.activation(ctile[:], ctile[:], AF.Silu),
                  waits=[(S["s_const"], v_const)], inc=act)
    v_ct = P.op("tensor", lambda e: e.transpose(ps_tf[:], ctile[:], identf[0:DC, 0:DC]),
                waits=[(act, v_silu), (S["s_const"], v_const)], inc=pe)
    v_sc = P.op("vector", lambda e: e.tensor_copy(siluc[:], ps_tf[:]),
                waits=[(pe, v_ct)], inc=dve)

    prev_evac = 0
    for sl in range(NSL):
        b = sl % 2
        sw0, sw1 = sl * SW, min(MODSH, (sl + 1) * SW)
        wid = sw1 - sw0
        wv_ = [(pe, wm_used[b])] if wm_used[b] else []
        vsl = P.op("sync", dma(
            wm_sl[b][:, 0:DC * wid].rearrange("p (c n) -> p c n", n=wid),
            wmod_in[:, sw0:sw1].rearrange("(c p) n -> p c n", p=128)),
            waits=wv_, inc=S["s_wma" if b == 0 else "s_wmb"], inc_k=16)

        def mv(e, b=b, wid=wid):
            ins = None
            pb = ps_mv[0:1, (b % 2) * 512:(b % 2) * 512 + wid]
            for dc in range(DC):
                ins = e.matmul(pb, siluc[:, dc:dc + 1],
                               wm_sl[b][:, dc * wid:(dc + 1) * wid],
                               start=(dc == 0), stop=(dc == DC - 1))
            return ins
        vmm = P.op("tensor", mv,
                   waits=[(S["s_wma" if b == 0 else "s_wmb"], vsl),
                          (dve, max(v_sc, prev_evac))], inc=pe)
        wm_used[b] = vmm
        prev_evac = P.op("vector", lambda e, sw0=sw0, sw1=sw1, b=b: e.tensor_copy(
            rowbuf[0:1, sw0:sw1], ps_mv[0:1, (b % 2) * 512:(b % 2) * 512 + (sw1 - sw0)]),
            waits=[(pe, vmm)], inc=dve)

    v_modl = P.op("gpsimd", dma(modl_d[:, :], rowbuf[0:1, 0:MODSH]),
                  waits=[(dve, prev_evac)], inc=S["s_modg"], inc_k=16)
    v_cc = P.op("gpsimd", lambda e: e.collective_compute(
        "AllGather", ALU.bypass, replica_groups=[list(range(cfg["n_cores"]))],
        ins=[modl_d.ap()], outs=[modg_d.ap()]),
        waits=[(S["s_modg"], v_modl)], inc=S["s_cc"])

    v_tr_prev = 0
    v_ev_prev = v_sc
    for idx, dst, addc in [(0, s1c, 0.0), (1, g1c, 1.0), (3, s2c, 0.0), (4, g2c, 1.0)]:
        wv_ = [(S["s_cc"], v_cc)]
        if v_tr_prev:
            wv_.append((pe, v_tr_prev))
        vld = P.op("sync", dma(
            mslice[:], mod_flat[idx * D:(idx + 1) * D].rearrange("(a p) -> a p", p=128)),
            waits=wv_, inc=S["s_mod"], inc_k=16)
        wv_ = [(S["s_mod"], vld)]
        if v_ev_prev:
            wv_.append((dve, v_ev_prev))
        v_tr_prev = P.op("tensor", lambda e: e.transpose(
            ps_tf[:], mslice[0:DC, :], identf[0:DC, 0:DC]), waits=wv_, inc=pe)
        if addc != 0.0:
            fn = lambda e, dst=dst, a=addc: e.tensor_scalar(dst[:], ps_tf[:], a, None, ALU.add)
        else:
            fn = lambda e, dst=dst: e.tensor_copy(dst[:], ps_tf[:])
        v_ev_prev = P.op("vector", fn, waits=[(pe, v_tr_prev)], inc=dve)
    v_modcols = v_ev_prev

    v_bias = P.op("sync", dma(rowbuf2[0:1, 0:D], bias_in), inc=S["s_row"], inc_k=16)

    def clamp_row(idx):
        vld = P.op("sync", dma(rowbuf[0:1, 0:D], mod_flat[idx * D:(idx + 1) * D]),
                   waits=[(S["s_cc"], v_cc), (dve, dve.n), (S["s_row"], S["s_row"].n)],
                   inc=S["s_row"], inc_k=16)
        P.op("vector", lambda e: e.tensor_scalar(
            scrr_a[0:1, 0:D], rowbuf[0:1, 0:D], -1.0, None, ALU.mult),
            waits=[(S["s_row"], vld)], inc=dve)
        P.op("vector", lambda e: e.tensor_tensor(
            scrr_a[0:1, 0:D], scrr_a[0:1, 0:D], rowbuf[0:1, 0:D], ALU.max), inc=dve)
        P.op("vector", lambda e: e.tensor_scalar(
            scrr_a[0:1, 0:D], scrr_a[0:1, 0:D], 0.25, None, ALU.max), inc=dve)
        P.op("vector", lambda e: e.tensor_scalar(
            scrr_b[0:1, 0:D], rowbuf[0:1, 0:D], 0.0, None, ALU.is_ge), inc=dve)
        P.op("vector", lambda e: e.tensor_scalar(
            scrr_b[0:1, 0:D], scrr_b[0:1, 0:D], 2.0, -1.0, ALU.mult, ALU.add), inc=dve)
        return P.op("vector", lambda e: e.tensor_tensor(
            rowbuf[0:1, 0:D], scrr_a[0:1, 0:D], scrr_b[0:1, 0:D], ALU.mult), inc=dve)

    vga = clamp_row(2)
    P.op("sync", dma(rows_d[0:1, :], rowbuf[0:1, 0:D]),
         waits=[(dve, vga)], inc=S["s_row"], inc_k=16)
    vgb = clamp_row(5)
    v_gb_st = P.op("sync", dma(rows_d[1:2, :], rowbuf[0:1, 0:D]),
                   waits=[(dve, vgb)], inc=S["s_row"], inc_k=16)
    v_gbb = P.op("vector", lambda e: e.tensor_tensor(
        rowbuf[0:1, 0:D], rowbuf[0:1, 0:D], rowbuf2[0:1, 0:D], ALU.mult),
        waits=[(S["s_row"], v_bias), (S["s_row"], v_gb_st)], inc=dve)
    P.op("sync", dma(rows_d[2:3, :], rowbuf[0:1, 0:D]),
         waits=[(dve, v_gbb)], inc=S["s_row"], inc_k=16)
    v_rows_done = S["s_row"].n

    P.flush()
    f0.close()
    if int(os.environ.get("DK_PHASES", "5")) <= 1:
        es.close()
        return nc

    f12 = ExitStack()
    h_fm = sbuf(f12, "h_fm", [128, DC * TCTX], BF16)
    costab = sbuf(f12, "costab_s", [128, TCTX], F32)
    sintab = sbuf(f12, "sintab_s", [128, TCTX], F32)
    halo_b = sbuf(f12, "halo_b", [128, NJT], F32)
    tri_lo = sbuf(f12, "tri_lo_s", [128, 128], BF16)
    tri_hi = sbuf(f12, "tri_hi_s", [128, 128], BF16)
    ones_c = sbuf(f12, "ones_c", [128, 128], BF16)
    RSW = min(1024, TCTX)
    scr_a = sbuf(f12, "scr_a", [128, RSW], F32)
    scr_b = sbuf(f12, "scr_b", [128, RSW], F32)
    tbl_w = [(pe, pe.n), (dve, dve.n), (S["s_row"], v_rows_done),
             (S["s_mod"], S["s_mod"].n), (S["s_modg"], S["s_modg"].n)]
    for dst, srcap in [
            (costab[:], cos_in), (sintab[:], sin_in), (halo_b[:], hb_in),
            (tri_lo[:], trilo_in), (tri_hi[:], trihi_in), (ones_c[:], ones_in)]:
        P.op("sync", dma(dst, srcap), waits=tbl_w, inc=S["s_const"], inc_k=16)
        tbl_w = []
    v_const2 = S["s_const"].n
    f1 = ExitStack()
    ps_t = [f1.enter_context(nc.psum_tensor(f"ps_t{i}", [128, 128], BF16)) for i in range(2)]
    xsl = [sbuf(f1, f"xsl{i}", [128, D], F32) for i in range(2)]
    xnsl = [sbuf(f1, f"xn{i}", [128, D], BF16) for i in range(2)]
    sqscr = sbuf(f1, "sqscr", [128, D], F32)

    def emit_ln(n_tiles, gcol, scol, out_fm, out_stride, x_ap_fn, x_wait_fn,
                x_free_sink, ps_tr, xn_bufs, sq_buf, tile_ready):
        tr_done = [0, 0]
        ev_done = [0, 0]
        xn_free = [0, 0]
        for tt in range(n_tiles):
            b = tt % 2
            xa = x_ap_fn(tt)
            xw = x_wait_fn(tt)
            P.op("vector", lambda e, xa=xa: e.reduce_sum(
                stat[:, 0:1], xa, axis=AX, negate=True), waits=xw, inc=dve)
            P.op("vector", lambda e: e.tensor_scalar(
                stat[:, 1:2], stat[:, 0:1], 1.0 / D, None, ALU.mult), inc=dve)
            va = P.op("scalar", lambda e, xa=xa: e.activation(
                sq_buf[:], xa, AF.Square, accum_out=stat[:, 2:3]),
                waits=xw + [(dve, dve.n)], inc=act)
            P.op("vector", lambda e: e.tensor_tensor(
                stat[:, 3:4], stat[:, 1:2], stat[:, 1:2], ALU.mult),
                waits=[(act, va)], inc=dve)
            P.op("vector", lambda e: e.tensor_scalar(
                stat[:, 4:5], stat[:, 2:3], 1.0 / D, eps, ALU.mult, ALU.add), inc=dve)
            v2 = P.op("vector", lambda e: e.tensor_tensor(
                stat[:, 4:5], stat[:, 4:5], stat[:, 3:4], ALU.subtract), inc=dve)
            P.op("scalar", lambda e: e.activation(
                stat[:, 5:6], stat[:, 4:5], AF.Ln), waits=[(dve, v2)], inc=act)
            v4 = P.op("scalar", lambda e: e.activation(
                stat[:, 6:7], stat[:, 5:6], AF.Exp, scale=-0.5), inc=act)
            wv_ = [(act, v4)]
            if xn_free[b]:
                wv_.append((pe, xn_free[b]))
            v5 = P.op("vector", lambda e, xa=xa, b=b: e.tensor_scalar(
                xn_bufs[b][:], xa, stat[:, 1:2], stat[:, 6:7], ALU.add, ALU.mult),
                waits=wv_, inc=dve)
            x_free_sink(tt, v5)
            for dc in range(DC):
                hb_ = dc % 2
                wv_ = [(dve, max(v5, ev_done[hb_]))]
                vt = P.op("tensor", lambda e, b=b, dc=dc, hb_=hb_: e.transpose(
                    ps_tr[hb_][:],
                    xn_bufs[b][:, dc * 128:(dc + 1) * 128], ident[:]),
                    waits=wv_, inc=pe)
                tr_done[hb_] = vt
                if dc == DC - 1:
                    xn_free[b] = vt
                vv = P.op("vector", lambda e, tt=tt, dc=dc, hb_=hb_: e.tensor_scalar(
                    out_fm[:, dc * out_stride + tt * 128: dc * out_stride + tt * 128 + 128],
                    ps_tr[hb_][:],
                    gcol[:, dc:dc + 1], scol[:, dc:dc + 1], ALU.mult, ALU.add),
                    waits=[(pe, vt)], inc=dve)
                ev_done[hb_] = vv
            tile_ready[tt] = ev_done[(DC - 1) % 2]

    barrier(["sync", "scalar"], [(pe, max(wm_used))])
    barrier(["vector"],
            [(pe, max(wm_used)), (S["s_row"], v_rows_done),
             (S["s_mod"], S["s_mod"].n), (S["s_modg"], S["s_modg"].n)])

    h_tile_ready = [0] * NJT
    x_free = [0, 0]

    def x1_src(tt):
        b = tt % 2
        wv_ = [(dve, x_free[b])] if x_free[b] else []
        S_ = S["s_xa" if b == 0 else "s_xb"]
        P.op("sync", dma(xsl[b][:], x_in[tt * 128:(tt + 1) * 128, :]),
             waits=wv_, inc=S_, inc_k=16)
        return xsl[b][:]

    x1_waits = {}

    def x1_ap(tt):
        ap = x1_src(tt)
        x1_waits[tt] = [(S["s_xa" if tt % 2 == 0 else "s_xb"],
                         S["s_xa" if tt % 2 == 0 else "s_xb"].n)]
        return ap

    emit_ln(NJT, g1c, s1c, h_fm, TCTX,
            x1_ap, lambda tt: x1_waits[tt],
            lambda tt, v: x_free.__setitem__(tt % 2, v),
            ps_t, xnsl, sqscr, h_tile_ready)
    P.flush()
    f1.close()
    if int(os.environ.get("DK_PHASES", "5")) <= 2:
        f12.close()
        es.close()
        return nc

    f2 = ExitStack()
    KQO = 0
    SO = 1024
    AVO = 2048
    DENO = AVO + ((TOWN + 511) // 512) * 512
    assert DENO + TOWN <= 4096
    arena = f2.enter_context(nc.psum_tensor("arena", [128, 4096], F32))

    krot = [sbuf(f2, f"krot{i}", [128, TCTX], BF16) for i in range(2)]
    qrot = [sbuf(f2, f"qrot{i}", [128, TOWN], BF16) for i in range(2)]
    pt_sl = [sbuf(f2, f"pt{i}", [128, PTW], BF16) for i in range(4)]
    v_sb = sbuf(f2, "v_sb", [128, NJT * VEW], BF16)
    wq_sl = [sbuf(f2, f"wqs{i}", [128, DC * 128], BF16) for i in range(2)]
    wk_sl = [sbuf(f2, f"wks{i}", [128, DC * 128], BF16) for i in range(2)]
    wv_sl = sbuf(f2, "wv_sl", [128, DC * VEW], BF16)
    wqk_f = sbuf(f2, "wqk_f", [128, DC * 128], F32)
    wvh_f = sbuf(f2, "wvh_f", [128, DC * 256], F32)
    rec_bc = sbuf(f2, "rec_bc", [128, TOWN], F32)
    ao_sl = [sbuf(f2, f"ao{i}", [128, TOWN], BF16) for i in range(2)]
    facc = sbuf(f2, "facc", [128, DC * EGW], F32)

    barrier(["gpsimd"], [(dve, dve.n), (act, act.n)])

    v_fprev = 0
    for eg in range(D // EGW):
        for r in range(R):
            wv_ = [(S["s_fga"], S["s_fga"].n)]
            if r == 0:
                wv_.append((S["s_fso"], S["s_fso"].n))
            vld = P.op("gpsimd", lambda e, r=r, eg=eg: e.dma_start(
                out=facc[:].rearrange("p (c n) -> p c n", n=EGW),
                in_=fac_in[r, :, eg * EGW:(eg + 1) * EGW].rearrange("(c p) n -> p c n", p=128),
                accum_op=(ALU.bypass if r == 0 else ALU.add)),
                waits=wv_, inc=S["s_fga"], inc_k=16, dma=True)
        vst = P.op("gpsimd", dma(
            fsum_d[:, eg * EGW:(eg + 1) * EGW].rearrange("(c p) n -> p c n", p=128),
            facc[:].rearrange("p (c n) -> p c n", n=EGW)),
            waits=[(S["s_fga"], S["s_fga"].n), (S["s_fso"], S["s_fso"].n)],
            inc=S["s_fso"], inc_k=16, dma=True)
        v_fprev = 0
    v_fsum_all = S["s_fso"].n

    wq_used = [0, 0]
    wk_used = [0, 0]
    wv_used = 0
    qk_stage_free = [0]
    qk_cast_done = {}

    def prefetch_qk(h):
        b = h % 2
        wv_ = [(act, qk_stage_free[0])] if qk_stage_free[0] else []
        vq_d = P.op("sync", dma(
            wqk_f[:].rearrange("p (c n) -> p c n", n=128),
            wq_in[:, h * 128:(h + 1) * 128].rearrange("(c p) n -> p c n", p=128)),
            waits=wv_, inc=S["s_wqa"], inc_k=16)
        cw = [(S["s_wqa"], vq_d)]
        if wq_used[b]:
            cw.append((pe, wq_used[b]))
        vq_c = P.op("scalar", lambda e, b=b: e.copy(wq_sl[b][:], wqk_f[:]),
                    waits=cw, inc=act)
        vk_d = P.op("sync", dma(
            wqk_f[:].rearrange("p (c n) -> p c n", n=128),
            wk_in[:, h * 128:(h + 1) * 128].rearrange("(c p) n -> p c n", p=128)),
            waits=[(act, vq_c)], inc=S["s_wqa"], inc_k=16)
        cw = [(S["s_wqa"], vk_d)]
        if wk_used[b]:
            cw.append((pe, wk_used[b]))
        vk_c = P.op("scalar", lambda e, b=b: e.copy(wk_sl[b][:], wqk_f[:]),
                    waits=cw, inc=act)
        qk_stage_free[0] = vk_c
        qk_cast_done[h] = (vq_c, vk_c)

    prefetch_qk(0)
    prefetch_qk(1)
    ao_stored = [0, 0]
    v_sb_ready = 0
    exp_prev_last = 0
    denav_prev_pe = 0
    denav_prev_dve = 0
    pt_consumed = [0] * 4
    rec_read_prev = 0
    den_dma_prev = 0

    sconsumed = [0, 0, 0]

    wv_cast_done = 0
    wv_prefetched = {}

    def prefetch_wv(g):
        nonlocal wv_cast_done, wv_used
        vwv = 0
        for half in range(VEW // 256):
            wv_ = [(act, wv_cast_done)] if wv_cast_done else []
            c0 = g * VEW + half * 256
            vd = P.op("sync", dma(
                wvh_f[:].rearrange("p (c n) -> p c n", n=256),
                wv_in[:, c0:c0 + 256].rearrange("(c p) n -> p c n", p=128)),
                waits=wv_, inc=S["s_wv"], inc_k=16)
            cw = [(S["s_wv"], vd)]
            if wv_used:
                cw.append((pe, wv_used))
            wv_cast_done = P.op("scalar", lambda e, half=half: e.copy(
                wv_sl[:].rearrange("p (c n) -> p c n", n=VEW)[:, :, half * 256:(half + 1) * 256],
                wvh_f[:].rearrange("p (c n) -> p c n", n=256)),
                waits=cw, inc=act)
        wv_prefetched[g] = wv_cast_done

    prefetch_wv(0)

    kq_bank_free = [0, 0]
    rope_done_k = {}
    rope_done_q = {}
    rope_all = {}
    smm_last_of = {}

    def emit_kqrope(hh):
        vwq, vwk = qk_cast_done[hh]
        rope_done_k[hh] = {}
        rope_done_q[hh] = {}
        guard = [(pe, smm_last_of[hh - 2])] if (hh - 2) in smm_last_of else []
        cidx = 0
        v = 0
        for kind, src_off, width, castv, out, book, cs_off in (
                ("k", 0, TCTX, vwk, krot[hh % 2], rope_done_k[hh], 0),
                ("q", W, TOWN, vwq, qrot[hh % 2], rope_done_q[hh], W)):
            wsl = wk_sl[hh % 2] if kind == "k" else wq_sl[hh % 2]
            for (n0, n1) in splits(0, width, 512):
                bank = cidx % 2
                cidx += 1
                wid = n1 - n0

                def pmm(e, n0=n0, n1=n1, wsl=wsl, src_off=src_off, bank=bank):
                    ins = None
                    for dc in range(DC):
                        ins = e.matmul(
                            arena[:, KQO + bank * 512:KQO + bank * 512 + (n1 - n0)],
                            wsl[:, dc * 128:(dc + 1) * 128],
                            h_fm[:, dc * TCTX + src_off + n0: dc * TCTX + src_off + n1],
                            start=(dc == 0), stop=(dc == DC - 1))
                    return ins
                wts = [(act, castv), (dve, max(h_tile_ready))]
                if kq_bank_free[bank]:
                    wts.append((dve, kq_bank_free[bank]))
                vmmv = P.op("tensor", pmm, waits=wts, inc=pe,
                            tag=f"kq h{hh} {kind} {n0}")
                if kind == "k":
                    wk_used[hh % 2] = vmmv
                else:
                    wq_used[hh % 2] = vmmv
                pv = arena[:, KQO + bank * 512:KQO + bank * 512 + wid]
                wts2 = [(pe, vmmv), (S["s_const"], v_const2)] + guard
                P.op("vector", lambda e, pv=pv, n0=n0, wid=wid, cs_off=cs_off: e.tensor_tensor(
                    scr_a[:, 0:wid], pv,
                    costab[:, cs_off + n0:cs_off + n0 + wid], ALU.mult), waits=wts2)
                P.op("vector", lambda e, pv=pv, n0=n0, wid=wid, cs_off=cs_off: e.tensor_tensor(
                    scr_b[0:64, 0:wid], pv[64:128, :],
                    sintab[0:64, cs_off + n0:cs_off + n0 + wid], ALU.mult))
                P.op("vector", lambda e, pv=pv, n0=n0, wid=wid, cs_off=cs_off: e.tensor_tensor(
                    scr_b[64:128, 0:wid], pv[0:64, :],
                    sintab[64:128, cs_off + n0:cs_off + n0 + wid], ALU.mult))
                v = P.op("vector", lambda e, out=out, n0=n0, wid=wid: e.tensor_tensor(
                    out[:, n0:n0 + wid], scr_a[:, 0:wid], scr_b[:, 0:wid], ALU.add),
                    tag=f"rope h{hh} {kind} {n0}")
                book[n0] = v
                kq_bank_free[bank] = v
        rope_all[hh] = v

    emit_kqrope(0)
    if H > 2:
        prefetch_qk(2)

    for h in range(H):
        g, hg = divmod(h, HG)
        if hg == 0:
            vwv = wv_prefetched[g]
            v_ev = 0
            for jt in range(NJT):
                sb_ = jt % 2

                def vmm(e, jt=jt, sb_=sb_):
                    ins = None
                    for dc in range(DC):
                        ins = e.matmul(arena[:, SO + sb_ * 512:SO + sb_ * 512 + VEW],
                                       h_fm[:, dc * TCTX + jt * 128: dc * TCTX + jt * 128 + 128],
                                       wv_sl[:, dc * VEW:(dc + 1) * VEW],
                                       start=(dc == 0), stop=(dc == DC - 1))
                    return ins
                wv2 = [(act, vwv), (dve, h_tile_ready[jt]),
                       (act, max(sconsumed[sb_], exp_prev_last if jt < 2 else 0))]
                if jt >= 2:
                    wv2.append((dve, v_ev))
                vp = P.op("tensor", vmm, waits=wv2, inc=pe, tag=f"vmm g{g} jt{jt}")
                wv_used = vp
                v_ev = P.op("vector", lambda e, jt=jt, sb_=sb_: e.tensor_copy(
                    v_sb[:, jt * VEW:(jt + 1) * VEW], arena[:, SO + sb_ * 512:SO + sb_ * 512 + VEW]),
                    waits=[(pe, vp)], inc=dve)
            v_sb_ready = v_ev

        if h + 1 < H:
            emit_kqrope(h + 1)

        si = 0
        den_started = [False] * max(1, TOWN // 512)
        av_started = [False] * max(1, TOWN // 512)
        last_pe = 0
        pending = None

        _order = [W // 128] + [j for j in range(NJT) if j != W // 128]
        _chunk_seq = []
        for _jt in _order:
            _j0 = _jt * 128
            _A0 = max(0, _j0 - W)
            _A1 = min(TOWN, _j0 + 128)
            for (_a0, _a1) in splits(_A0, _A1, 512):
                _chunk_seq.append((_jt, _a0, _a1))
        _last_of_bank = {}
        for _i, (_jt, _a0, _a1) in enumerate(_chunk_seq):
            _last_of_bank[_a0 // 512] = _i
        _seq_idx = [0]

        def emit_denav(jt, A0, chunks, v_exp, v_tri, pos=None, h=h, hg=hg):
            nonlocal last_pe
            ptl = pt_sl[pos % 4]
            for (a0, a1) in chunks:
                di = a0 // 512
                idx = _seq_idx[0]
                _seq_idx[0] += 1
                assert _chunk_seq[idx] == (jt, a0, a1)
                isl = (_last_of_bank[di] == idx)

                def dmm(e, a0=a0, a1=a1, ptl=ptl, A0=A0,
                        st=(not den_started[di]), sp=isl):
                    return e.matmul(arena[:, DENO + a0:DENO + a1],
                                    ones_c[:], ptl[:, a0 - A0:a1 - A0],
                                    start=st, stop=sp)
                wts = [(act, v_exp), (dve, v_tri)]
                if h > 0 and not any(den_started) and not any(av_started):
                    wts.append((dve, denav_prev_dve))
                P.op("tensor", dmm, waits=wts, inc=pe, tag=f"den h{h} jt{jt} q{a0}")
                den_started[di] = True

                def amm(e, jt=jt, A0=A0, a0=a0, a1=a1, hg=hg, ptl=ptl,
                        st=(not av_started[di]), sp=isl):
                    return e.matmul(arena[:, AVO + a0:AVO + a1],
                                    v_sb[:, jt * VEW + hg * 128: jt * VEW + hg * 128 + 128],
                                    ptl[:, a0 - A0:a1 - A0], start=st, stop=sp)
                wts2 = [(dve, v_sb_ready), (act, v_exp), (dve, v_tri)]
                if h > 0 and not any(av_started):
                    wts2.append((dve, denav_prev_dve))
                last_pe = P.op("tensor", amm, waits=wts2, inc=pe,
                               tag=f"av h{h} jt{jt} q{a0}")
                av_started[di] = True

        for pos, jt in enumerate(_order):
            j0 = jt * 128
            A0 = max(0, j0 - W)
            A1 = min(TOWN, j0 + 128)
            if pos == 0:
                assert (A0, A1) == (0, TOWN)
            chunks = splits(A0, A1, 512)
            ptl = pt_sl[pos % 4]
            s_mm = []
            for (a0, a1) in chunks:
                slx = si % 2
                si += 1

                def smm(e, jt=jt, a0=a0, a1=a1, slx=slx, h=h):
                    return e.matmul(arena[:, SO + slx * 512:SO + slx * 512 + (a1 - a0)],
                                    krot[h % 2][:, jt * 128:jt * 128 + 128],
                                    qrot[h % 2][:, a0:a1], start=True, stop=True)
                wts = [(dve, rope_done_k[h][(jt * 128 // 512) * 512]),
                       (dve, rope_done_q[h][(a0 // 512) * 512])]
                if sconsumed[slx]:
                    wts.append((act, sconsumed[slx]))
                if hg == 0:
                    wts.append((dve, v_sb_ready))
                vs = P.op("tensor", smm, waits=wts, inc=pe,
                          tag=f"smm h{h} jt{jt} a{a0}")
                s_mm.append((a0, a1, vs, slx))
            smm_last = s_mm[-1][2]
            vexp = 0
            for (a0, a1, vs, slx) in s_mm:
                def expf(e, jt=jt, a0=a0, a1=a1, slx=slx, A0=A0, ptl=ptl):
                    return e.activation(ptl[:, a0 - A0:a1 - A0],
                                        arena[:, SO + slx * 512:SO + slx * 512 + (a1 - a0)],
                                        AF.Exp, bias=halo_b[:, jt:jt + 1])
                wts = [(pe, vs)]
                if pt_consumed[pos % 4]:
                    wts.append((pe, pt_consumed[pos % 4]))
                vexp = P.op("scalar", expf, waits=wts, inc=act,
                            tag=f"exp h{h} jt{jt} a{a0}")
                sconsumed[slx] = vexp
            tris = []
            if j0 < TOWN:
                tris.append((j0 - A0, tri_lo))
            if j0 >= W:
                tris.append((0, tri_hi))
            vt_ = 0
            for toff, tmask in tris:
                vt_ = P.op("vector",
                           lambda e, toff=toff, tmask=tmask, ptl=ptl: e.tensor_tensor(
                               ptl[:, toff:toff + 128], ptl[:, toff:toff + 128],
                               tmask[:], ALU.mult),
                           waits=[(act, vexp)], inc=dve, tag=f"tri h{h} jt{jt}")
            if pending is not None:
                emit_denav(*pending[1:], pos=pending[0])
                pt_consumed[pending[0] % 4] = last_pe
            pending = (pos, jt, A0, chunks, vexp, vt_)
        smm_last_of[h] = smm_last
        if h + 3 < H:
            prefetch_qk(h + 3)
        if hg == 1 and g + 1 < H // HG:
            prefetch_wv(g + 1)
        emit_denav(*pending[1:], pos=pending[0])
        pt_consumed[pending[0] % 4] = last_pe
        pending = None
        exp_prev_last = sconsumed[(si - 1) % 2]

        wts = [(pe, last_pe)]
        if rec_read_prev:
            wts.append((dve, rec_read_prev))
        vdc = P.op("vector", lambda e: e.tensor_copy(
            rec_bc[:], arena[:, DENO:DENO + TOWN]), waits=wts, inc=dve)
        vdb = P.op("vector", lambda e: e.reciprocal(rec_bc[:], rec_bc[:]),
                   waits=[(dve, vdc)], inc=dve)

        b = h % 2
        wts = [(pe, last_pe), (dve, vdb)]
        if ao_stored[b]:
            wts.append((S["s_aoa" if b == 0 else "s_aob"], ao_stored[b]))
        vev = P.op("vector", lambda e, b=b: e.scalar_tensor_tensor(
            ao_sl[b][:], arena[:, AVO:AVO + TOWN], 1.0, rec_bc[:], ALU.mult, ALU.mult),
            waits=wts, inc=dve)
        rec_read_prev = vev
        denav_prev_dve = vev
        ao_stored[b] = P.op("sync", dma(ao_d[h * 128:(h + 1) * 128, :], ao_sl[b][:]),
                            waits=[(dve, vev)],
                            inc=S["s_aoa" if b == 0 else "s_aob"], inc_k=16)

    v_ao_all = (S["s_aoa"].n, S["s_aob"].n)
    P.flush()
    f2.close()
    f12.close()
    if int(os.environ.get("DK_PHASES", "5")) <= 3:
        es.close()
        return nc

    f34 = ExitStack()
    x1_sb = sbuf(f34, "x1_sb", [128, NQT * D], F32)
    h2_fm = sbuf(f34, "h2_fm", [128, DC * TOWN], BF16)
    gb_b = sbuf(f34, "gb_b", [128, D], BF16)
    gbb_b = sbuf(f34, "gbb_b", [128, D], BF16)
    f3 = ExitStack()
    ps_t3 = [f3.enter_context(nc.psum_tensor(f"ps_t3{i}", [128, 128], BF16)) for i in range(2)]
    ps_wo = f3.enter_context(nc.psum_tensor("ps_wo", [128, 1024], F32))
    ao_sb = sbuf(f3, "ao_sb", [128, DC * TOWN], BF16)
    wo_sl = [sbuf(f3, f"wo_sl{i}", [128, DC * 512], BF16) for i in range(2)]
    wo_f = sbuf(f3, "wo_f", [128, DC * 256], F32)
    ga_b = sbuf(f3, "ga_b", [128, D], BF16)
    xsl3 = [sbuf(f3, f"x3_{i}", [128, 512], F32) for i in range(2)]
    xnsl3 = [sbuf(f3, f"xn3_{i}", [128, D], BF16) for i in range(2)]
    sqscr3 = sbuf(f3, "sqscr3", [128, D], BF16)
    t1scr = sbuf(f3, "t1scr", [128, 512], F32)

    barrier(["sync", "gpsimd", "tensor", "vector", "scalar"],
            [(pe, pe.n), (dve, dve.n), (act, act.n), (gp, gp.n),
             (S["s_fso"], v_fsum_all), (S["s_aoa"], v_ao_all[0]),
             (S["s_aob"], v_ao_all[1]),
             (S["s_den"], S["s_den"].n), (S["s_denb"], S["s_denb"].n)])

    P.op("gpsimd", dma(ga_b[:], rows_d[0:1, :].broadcast_to([128, D])),
         waits=[(S["s_row"], v_rows_done)], inc=S["s_rowg"], inc_k=16, dma=True)
    P.op("gpsimd", dma(gb_b[:], rows_d[1:2, :].broadcast_to([128, D])),
         inc=S["s_rowg"], inc_k=16, dma=True)
    P.op("gpsimd", dma(gbb_b[:], rows_d[2:3, :].broadcast_to([128, D])),
         inc=S["s_rowg"], inc_k=16, dma=True)
    vgb_b = vgbb_b = None
    v_aor = P.op("sync", dma(ao_sb[:].rearrange("p (c n) -> p c n", n=TOWN),
                             ao_d.ap().rearrange("(c p) n -> p c n", p=128)),
                 waits=[(S["s_aoa"], v_ao_all[0]), (S["s_aob"], v_ao_all[1])],
                 inc=S["s_aor"], inc_k=16)

    NDG = D // 512
    wo_used = 0
    x1_ready = [0] * NQT
    h2_tile_ready = [0] * NQT
    _ln2_state = dict(tr=[0, 0], ev=[0, 0], xf=[0, 0])

    def emit_ln2_tile(tt):
        b = tt % 2
        xa = x1_sb[:, tt * D:(tt + 1) * D]
        xw = [(dve, x1_ready[tt])]
        P.op("vector", lambda e, xa=xa: e.reduce_sum(
            stat[:, 0:1], xa, axis=AX, negate=True), waits=xw, inc=dve)
        P.op("vector", lambda e: e.tensor_scalar(
            stat[:, 1:2], stat[:, 0:1], 1.0 / D, None, ALU.mult), inc=dve)
        va = P.op("scalar", lambda e, xa=xa: e.activation(
            sqscr3[:], xa, AF.Square, accum_out=stat[:, 2:3]),
            waits=xw + [(dve, dve.n)], inc=act)
        P.op("vector", lambda e: e.tensor_tensor(
            stat[:, 3:4], stat[:, 1:2], stat[:, 1:2], ALU.mult),
            waits=[(act, va)], inc=dve)
        P.op("vector", lambda e: e.tensor_scalar(
            stat[:, 4:5], stat[:, 2:3], 1.0 / D, eps, ALU.mult, ALU.add), inc=dve)
        v2 = P.op("vector", lambda e: e.tensor_tensor(
            stat[:, 4:5], stat[:, 4:5], stat[:, 3:4], ALU.subtract), inc=dve)
        P.op("scalar", lambda e: e.activation(
            stat[:, 5:6], stat[:, 4:5], AF.Ln), waits=[(dve, v2)], inc=act)
        v4 = P.op("scalar", lambda e: e.activation(
            stat[:, 6:7], stat[:, 5:6], AF.Exp, scale=-0.5), inc=act)
        wv_ = [(act, v4)]
        if _ln2_state["xf"][b]:
            wv_.append((pe, _ln2_state["xf"][b]))
        v5 = P.op("vector", lambda e, xa=xa, b=b: e.tensor_scalar(
            xnsl3[b][:], xa, stat[:, 1:2], stat[:, 6:7], ALU.add, ALU.mult),
            waits=wv_, inc=dve)
        for dc in range(DC):
            hb_ = dc % 2
            wv_ = [(dve, max(v5, _ln2_state["ev"][hb_]))]
            vt = P.op("tensor", lambda e, b=b, dc=dc, hb_=hb_: e.transpose(
                ps_t3[hb_][:],
                xnsl3[b][:, dc * 128:(dc + 1) * 128], ident[:]),
                waits=wv_, inc=pe)
            _ln2_state["tr"][hb_] = vt
            if dc == DC - 1:
                _ln2_state["xf"][b] = vt
            vv = P.op("vector", lambda e, tt=tt, dc=dc, hb_=hb_: e.tensor_scalar(
                h2_fm[:, dc * TOWN + tt * 128: dc * TOWN + tt * 128 + 128],
                ps_t3[hb_][:],
                g2c[:, dc:dc + 1], s2c[:, dc:dc + 1], ALU.mult, ALU.add),
                waits=[(pe, vt)], inc=dve)
            _ln2_state["ev"][hb_] = vv
        h2_tile_ready[tt] = _ln2_state["ev"][(DC - 1) % 2]
    x3_done = [0, 0]
    x3_loaded = [0, 0]
    wo_evac = [0, 0]
    wo_cast_done = 0
    wo_cast_of = {}
    wo_mm_of = {}

    def prefetch_wo(dg):
        nonlocal wo_cast_done
        for half in range(2):
            wv_ = [(act, wo_cast_done)] if wo_cast_done else []
            c0 = dg * 512 + half * 256
            vd = P.op("sync", dma(
                wo_f[:].rearrange("p (c n) -> p c n", n=256),
                wo_in[:, c0:c0 + 256].rearrange("(c p) n -> p c n", p=128)),
                waits=wv_, inc=S["s_wo"], inc_k=16)
            cw = [(S["s_wo"], vd)]
            if dg - 2 in wo_mm_of:
                cw.append((pe, wo_mm_of[dg - 2]))
            wo_cast_done = P.op("scalar", lambda e, half=half, dg=dg: e.copy(
                wo_sl[dg % 2][:].rearrange("p (c n) -> p c n", n=512)[:, :, half * 256:(half + 1) * 256],
                wo_f[:].rearrange("p (c n) -> p c n", n=256)),
                waits=cw, inc=act)
        wo_cast_of[dg] = wo_cast_done

    prefetch_wo(0)
    for dg in range(NDG):
        if dg + 1 < NDG:
            prefetch_wo(dg + 1)
        vwo = wo_cast_of[dg]
        for tt in range(NQT):
            bb = tt % 2
            wv_ = [(dve, x3_done[bb])] if x3_done[bb] else []
            v_x3 = P.op(
                "sync", dma(xsl3[bb][:],
                            x_in[W + tt * 128:W + (tt + 1) * 128, dg * 512:(dg + 1) * 512]),
                waits=wv_, inc=S["s_xoa" if bb == 0 else "s_xob"], inc_k=16)

            def womm(e, tt=tt, dg=dg):
                ins = None
                for ec in range(DC):
                    ins = e.matmul(ps_wo[:, (tt % 2) * 512:(tt % 2) * 512 + 512],
                                   ao_sb[:, ec * TOWN + tt * 128: ec * TOWN + tt * 128 + 128],
                                   wo_sl[dg % 2][:, ec * 512:(ec + 1) * 512],
                                   start=(ec == 0), stop=(ec == DC - 1))
                return ins
            wts = [(act, vwo), (S["s_aor"], v_aor)]
            if wo_evac[tt % 2]:
                wts.append((dve, wo_evac[tt % 2]))
            vwomm = P.op("tensor", womm, waits=wts, inc=pe)
            wo_used = vwomm
            wo_mm_of[dg] = vwomm

            wts = [(pe, vwomm), (S["s_xoa" if bb == 0 else "s_xob"], v_x3),
                   (S["s_rowg"], S["s_rowg"].n)]
            P.op("vector", lambda e, tt=tt, dg=dg: e.scalar_tensor_tensor(
                t1scr[:], ps_wo[:, (tt % 2) * 512:(tt % 2) * 512 + 512],
                1.0, ga_b[:, dg * 512:(dg + 1) * 512], ALU.mult, ALU.mult),
                waits=wts)
            wo_evac[tt % 2] = P.op("vector", lambda e, tt=tt, dg=dg: e.tensor_tensor(
                x1_sb[:, tt * D + dg * 512: tt * D + (dg + 1) * 512],
                t1scr[:], xsl3[tt % 2][:], ALU.add))
            x3_done[bb] = wo_evac[tt % 2]
            if dg == NDG - 1:
                x1_ready[tt] = wo_evac[tt % 2]
                emit_ln2_tile(tt)

    P.flush()
    f3.close()
    if int(os.environ.get("DK_PHASES", "5")) <= 4:
        f34.close()
        es.close()
        return nc

    f4 = ExitStack()
    NEGF = D // 512
    ps_ff = f4.enter_context(nc.psum_tensor("ps_ff", [128, 512 * NEGF], F32))
    f_sb = sbuf(f4, "f_sb", [128, DC * D], BF16)
    out_sl = [sbuf(f4, f"out_sl{i}", [128, D], F32) for i in range(2)]
    t1scr4 = sbuf(f4, "t1scr4", [128, 512], F32)

    barrier(["sync", "tensor", "vector", "scalar"],
            [(pe, pe.n), (dve, dve.n), (act, act.n),
             (S["s_xoa"], S["s_xoa"].n), (S["s_xob"], S["s_xob"].n),
             (S["s_aor"], S["s_aor"].n)])

    for sl4 in range(4):
        c0 = sl4 * (D // 4)
        P.op("sync", dma(
            f_sb[:].rearrange("p (c n) -> p c n", n=D)[:, :, c0:c0 + D // 4],
            fsum_d.ap()[:, c0:c0 + D // 4].rearrange("(c p) n -> p c n", p=128)),
            waits=[(S["s_fso"], v_fsum_all)], inc=S["s_fr"], inc_k=16)
    v_fr = S["s_fr"].n

    x1p_ready = [0] * NQT
    for tt in range(NQT):
        x1p_ready[tt] = P.op("vector", lambda e, tt=tt: e.tensor_tensor(
            x1_sb[:, tt * D:(tt + 1) * D], x1_sb[:, tt * D:(tt + 1) * D],
            gbb_b[:], ALU.add),
            waits=[(dve, h2_tile_ready[tt]), (S["s_rowg"], S["s_rowg"].n)], inc=dve)

    ff_evac = 0
    out_dma_of = [0, 0]
    for tt in range(NQT):
        def ffmm(e, tt=tt):
            ins = None
            for dc in range(DC):
                for eg in range(NEGF):
                    ins = e.matmul(ps_ff[:, eg * 512:(eg + 1) * 512],
                                   h2_fm[:, dc * TOWN + tt * 128: dc * TOWN + tt * 128 + 128],
                                   f_sb[:, dc * D + eg * 512:dc * D + (eg + 1) * 512],
                                   start=(dc == 0), stop=(dc == DC - 1))
            return ins
        wts = [(S["s_fr"], v_fr), (dve, h2_tile_ready[tt])]
        if ff_evac:
            wts.append((dve, ff_evac))
        vf = P.op("tensor", ffmm, waits=wts, inc=pe)

        last_ev = 0
        ob = tt % 2
        for eg in range(NEGF):
            wts = [(pe, vf), (dve, x1p_ready[tt]), (S["s_rowg"], S["s_rowg"].n)]
            if eg == 0 and out_dma_of[ob]:
                wts.append((S["s_ya" if ob == 0 else "s_yb"], out_dma_of[ob]))
            P.op("vector", lambda e, eg=eg: e.scalar_tensor_tensor(
                t1scr4[:], ps_ff[:, eg * 512:(eg + 1) * 512],
                1.0 / math.sqrt(R), gb_b[:, eg * 512:(eg + 1) * 512],
                ALU.mult, ALU.mult),
                waits=wts)
            last_ev = P.op("vector", lambda e, tt=tt, eg=eg, ob=ob: e.tensor_tensor(
                out_sl[ob][:, eg * 512:(eg + 1) * 512], t1scr4[:],
                x1_sb[:, tt * D + eg * 512: tt * D + (eg + 1) * 512], ALU.add))
        ff_evac = last_ev
        out_dma_of[ob] = P.op("sync", dma(y_out[tt * 128:(tt + 1) * 128, :], out_sl[ob][:]),
                              waits=[(dve, last_ev)],
                              inc=S["s_ya" if ob == 0 else "s_yb"], inc_k=16)

    P.op("sync", lambda e: None,
         waits=[(S["s_ya"], S["s_ya"].n), (S["s_yb"], S["s_yb"].n)])
    P.flush()
    f4.close()
    f34.close()
    es.close()
    return nc


CFG = make_cfg()


def shard_inputs(cfg, full):
    D, TOWN, W, TCTX = cfg["D"], cfg["TOWN"], cfg["W"], cfg["TCTX"]
    n = cfg["n_cores"]
    x = np.ascontiguousarray(np.asarray(full["x"], dtype=np.float32)).reshape(-1, D)
    maps = []
    for i in range(n):
        t0 = i * TOWN - W
        if t0 < 0:
            xs = np.concatenate([np.zeros((-t0, D), np.float32), x[0:TCTX + t0]], 0)
        else:
            xs = x[t0:t0 + TCTX]
        tabs = host_tables(cfg, i)
        maps.append(dict(
            x=np.ascontiguousarray(xs),
            wq=full["wq"], wk=full["wk"], wv=full["wv"], wo=full["wo"],
            factors=full["factors"], fbias=full["fbias"].reshape(1, D),
            cvec=full["cvec"].reshape(1, D),
            wmod=np.ascontiguousarray(full["wmod"][:, i * cfg["MODSH"]:(i + 1) * cfg["MODSH"]]),
            costab=tabs["costab"], sintab=tabs["sintab"], halo_bias=tabs["halo_bias"],
            tri_lo=tabs["tri_lo"], tri_hi=tabs["tri_hi"], ident=tabs["ident"],
            identf=tabs["identf"], ones_col=tabs["ones_col"]))
    return maps


def run(cfg, full):
    maps = shard_inputs(cfg, full)
    nc = build_program(cfg)
    res = run_bass_kernel_spmd(nc, maps, core_ids=list(range(cfg["n_cores"])))
    y = np.concatenate([np.asarray(r["y"]) for r in res.results], axis=0)
    return y.reshape(1, cfg["n_cores"] * cfg["TOWN"], cfg["D"]).astype(np.float32)


def kernel(L0_attn_hidden_states, L0_attn_wq, L0_attn_wk, L0_attn_wv,
           L0_attn_wo, L0_ffn_factors, L0_ffn_bias, L0_c, L0_wmod):
    full = dict(x=L0_attn_hidden_states,
                wq=np.asarray(L0_attn_wq, np.float32),
                wk=np.asarray(L0_attn_wk, np.float32),
                wv=np.asarray(L0_attn_wv, np.float32),
                wo=np.asarray(L0_attn_wo, np.float32),
                factors=np.asarray(L0_ffn_factors, np.float32),
                fbias=np.asarray(L0_ffn_bias, np.float32),
                cvec=np.asarray(L0_c, np.float32),
                wmod=np.asarray(L0_wmod, np.float32))
    return run(CFG, full)
